# version 03
# baseline (speedup 1.0000x reference)
import numpy as np
import ml_dtypes
import concourse.bass as bass
import concourse.mybir as mybir
from concourse.bass_utils import run_bass_kernel_spmd
from contextlib import ExitStack

F32 = mybir.dt.float32
BF16 = mybir.dt.bfloat16
AF = mybir.ActivationFunctionType
ALU = mybir.AluOpType
AX = mybir.AxisListType

B, T, D, H, HD, E, FF = 4, 4096, 1024, 16, 64, 8, 4096
NOWN, NB = 2048, 16
EPS = 1e-5

ENGS = ("te", "ve", "se")
QUEUES = ("sy", "gp", "vq")
QENG = {"sy": "sy", "gp": "gp", "vq": "se"}


class Prog:

    def __init__(self):
        self.items = {k: [] for k in ("te", "ve", "se", "gp", "sy")}
        self.ticks = {e: 0 for e in ENGS}
        self.dmas = {q: 0 for q in QUEUES}
        self.clock = {k: {} for k in ("te", "ve", "se", "gp", "sy")}
        self.state = {}

    def _need(self, reads, writes):
        evs = []
        for k in reads:
            st = self.state.get(k)
            if st and st["w"] is not None:
                evs.append(st["w"])
        for k in writes:
            st = self.state.get(k)
            if st:
                if st["w"] is not None:
                    evs.append(st["w"])
                evs.extend(st["r"])
        return evs

    def _wait(self, stream, evs):
        cl = self.clock[stream]
        for kind, src, val, evcl in evs:
            key = (kind, src)
            if kind == "t" and src == stream == "te":
                continue
            if cl.get(key, 0) >= val:
                continue
            self.items[stream].append(("wait", key, val))
            for kk, vv in evcl.items():
                if cl.get(kk, 0) < vv:
                    cl[kk] = vv
            if cl.get(key, 0) < val:
                cl[key] = val

    def op(self, eng, fn, reads=(), writes=()):
        evs = self._need(reads, writes)
        self._wait(eng, evs)
        self.ticks[eng] += 1
        val = self.ticks[eng]
        self.items[eng].append(("call", fn, ("t", eng), 1))
        evcl = dict(self.clock[eng])
        evcl[("t", eng)] = val
        ev = ("t", eng, val, evcl)
        for k in writes:
            self.state[k] = {"w": ev, "r": []}
        for k in reads:
            st = self.state.setdefault(k, {"w": None, "r": []})
            st["r"].append(ev)
        return ev

    def dma(self, q, fn, reads=(), writes=()):
        stream = QENG[q]
        evs = self._need(reads, writes)
        self._wait(stream, evs)
        self.dmas[q] += 1
        val = 16 * self.dmas[q]
        self.items[stream].append(("call", fn, ("d", q), 16))
        evcl = dict(self.clock[stream])
        evcl[("d", q)] = val
        ev = ("d", q, val, evcl)
        for k in writes:
            self.state[k] = {"w": ev, "r": []}
        for k in reads:
            st = self.state.setdefault(k, {"w": None, "r": []})
            st["r"].append(ev)
        return ev

    def finish(self):
        for q in QUEUES:
            if self.dmas[q]:
                self._wait(QENG[q], [("d", q, 16 * self.dmas[q], {})])
        for q in QUEUES:
            if self.dmas[q]:
                self._wait("gp", [("d", q, 16 * self.dmas[q], {})])

    def emit(self, nc, sems):
        streams = {"te": "tensor", "ve": "vector", "se": "scalar",
                   "gp": "gpsimd", "sy": "sync"}
        with nc.Block() as block:
            for s, engname in streams.items():
                items = self.items[s]
                if not items:
                    continue
                dec = getattr(block, engname)

                def body(eng, items=items):
                    for it in items:
                        if it[0] == "wait":
                            eng.wait_ge(sems[it[1]], it[2])
                        else:
                            it[1](eng).then_inc(sems[it[2]], it[3])
                dec(body)


DBG = False


def build_program(nc):
    p = Prog()
    st = ExitStack()

    def din(name, shape, dt=F32):
        return nc.dram_tensor(name, list(shape), dt, kind="ExternalInput").ap()

    x = din("x", (T, D))
    Wqkv = din("Wqkv", (D, 3 * D))
    Wq = din("Wq", (D, D)); Wk = din("Wk", (D, D)); Wv = din("Wv", (D, D))
    Wo = din("Wo", (D, D)); Wproj = din("Wproj", (D, D))
    Wr = din("Wr", (D, E))
    ew1 = din("ew1", (E, D, FF)); ew2 = din("ew2", (E, FF, D))
    eb2 = din("eb2", (E, D))
    bqkv_c = din("bqkv_c", (128, 24))
    bq_c = din("bq_c", (128, 8)); bk_c = din("bk_c", (128, 8))
    l1g_c = din("l1g_c", (128, 8)); l1b_c = din("l1b_c", (128, 8))
    l2g_c = din("l2g_c", (128, 8)); l2b_c = din("l2b_c", (128, 8))
    b1cols = din("b1cols", (128, 256))
    bqkv_vrow = din("bqkv_vrow", (1, D)); bv_row = din("bv_row", (1, D))
    bo_row = din("bo_row", (1, D)); bproj_row = din("bproj_row", (1, D))
    br_row = din("br_row", (1, E))
    ident = din("ident", (128, 128))
    ones_colf = din("ones_colf", (128, 1))
    ones_rowf = din("ones_rowf", (1, 128))
    ones_bf = din("ones_bf", (1, 2048), BF16)
    ones128_bf = din("ones128_bf", (1, 128), BF16)
    masks_in = din("masks", (8, 128, 512), BF16)
    rsig_bc = din("rsig_bc", (128, 1))

    out = nc.dram_tensor("out", [NOWN, D], F32, kind="ExternalOutput").ap()

    def scr(name, shape, dt=BF16):
        return nc.dram_tensor(name, list(shape), dt, kind="Internal").ap()

    s_xnT = scr("s_xnT", (8, 128, T))
    s_hT = scr("s_hT", (8, 128, T))
    s_qr = scr("s_qr", (H, 66, T))
    s_kh = scr("s_kh", (H, 66, NOWN))
    s_k = scr("s_k", (H, 64, T))
    s_q = scr("s_q", (H, 64, NOWN))
    s_v = scr("s_v", (2, 32, 128, 512))
    s_vr = scr("s_vr", (2, 32, 128, 512))
    s_a = scr("s_a", (NB, 128, D), F32)

    NBF, NFA, NFB = 66560, 7168, 8192
    abf = st.enter_context(nc.sbuf_tensor("abf", [128, NBF], BF16))
    afa = st.enter_context(nc.sbuf_tensor("afa", [128, NFA], F32))
    afb = st.enter_context(nc.sbuf_tensor("afb", [128, NFB], F32))

    o_big0 = 0
    o_wsb = 32768
    o_own = 40960
    o_bfst = [45056, 45568]
    o_oT = 0
    o_kb = [16384, 20480]
    o_vb = [24576, 26656]
    o_qs = [28736, 29248]
    o_pbuf = [29760, 30784]
    o_tstage = [40960, 41984]
    o_masks = 40960
    o_mid = 0
    o_h2T = 32768
    o_w2b = 40960
    o_w1b = [57344, 58368]
    o_wT = 59392
    o_kh_b = [0, 2048]
    o_qh_b = [4096, 8192]
    o_vrb = [12288, 14336]
    o_orT = 16384
    o_eb = [41984, 42496]
    sm = 61440

    def bfslot(n):
        nonlocal sm
        o = sm; sm += n
        assert sm <= NBF
        return o

    o_ones128 = bfslot(128)
    o_rows = bfslot(1024)
    o_rows2 = bfslot(1024)
    o_brrow = bfslot(8)
    o_eb2 = bfslot(1024)
    o_Wr = bfslot(64)

    o_xb = [0, 1024]
    o_sq = 2048
    o_xc = 3072
    o_xn1 = 4096
    o_h1 = 5120
    fa = 6144

    def faslot(n):
        nonlocal fa
        o = fa; fa += n
        assert fa <= NFA
        return o

    o_bqkvc = faslot(24)
    o_bqc = faslot(8); o_bkc = faslot(8)
    o_l1g = faslot(8); o_l1b = faslot(8)
    o_l2g = faslot(8); o_l2b = faslot(8)
    o_b1c = faslot(256)
    o_ident = faslot(128)
    o_onescf = faslot(1)
    o_onesrf = faslot(128)
    o_rsig = faslot(1)
    o_rsig2 = faslot(1)
    o_nrsig = faslot(1)
    o_red = faslot(8)
    o_wrt = faslot(64)

    _pb_single = {i: st.enter_context(nc.psum_tensor(f"pb{i}", [128, 512],
                                                     F32))
                  for i in (0, 1, 4, 5, 6, 7)}
    pb23 = st.enter_context(nc.psum_tensor("pb23", [128, 1024], F32))
    pb = [_pb_single[0], _pb_single[1], pb23[0:128, 0:512],
          pb23[0:128, 512:1024], _pb_single[4], _pb_single[5],
          _pb_single[6], _pb_single[7]]
    PK = ["p0", "p1", "p2", "p3", "pv0", "pv1", "p6", "p7"]

    sems = {}
    for e in ENGS:
        sems[("t", e)] = st.enter_context(nc.semaphore(f"tick_{e}"))
    for q in QUEUES:
        sems[("d", q)] = st.enter_context(nc.semaphore(f"dma_{q}"))

    def B_(off, cols, p0=0, np_=128):
        return abf[p0:p0 + np_, off:off + cols]

    def FA(off, cols, p0=0, np_=128):
        return afa[p0:p0 + np_, off:off + cols]

    def FB(off, cols, p0=0, np_=128):
        return afb[p0:p0 + np_, off:off + cols]

    def ld(dst, src, key):
        p.dma("sy", lambda e, d=dst, s=src: e.dma_start(out=d, in_=s),
              reads=(), writes=(key,))

    ld(B_(o_ones128, 128, 0, 1), ones128_bf[0:1, :], "c_ones128")
    ld(B_(o_ones128, 128, 32, 1), ones128_bf[0:1, :], "c_ones128")
    ld(B_(o_ones128, 128, 64, 1), ones128_bf[0:1, :], "c_ones128")
    ld(FA(o_bqkvc, 24), bqkv_c[:, :], "c_bqkvc")
    ld(FA(o_bqc, 8), bq_c[:, :], "c_bqc")
    ld(FA(o_bkc, 8), bk_c[:, :], "c_bkc")
    ld(FA(o_l1g, 8), l1g_c[:, :], "c_l1g")
    ld(FA(o_l1b, 8), l1b_c[:, :], "c_l1b")
    ld(FA(o_l2g, 8), l2g_c[:, :], "c_l2g")
    ld(FA(o_l2b, 8), l2b_c[:, :], "c_l2b")
    ld(FA(o_b1c, 256), b1cols[:, :], "c_b1c")
    ld(FA(o_ident, 128), ident[:, :], "c_ident")
    ld(FA(o_onescf, 1), ones_colf[:, :], "c_onescf")
    ld(FA(o_onesrf, 128, 0, 1), ones_rowf[0:1, :], "c_onesrf")
    ld(FA(o_onesrf, 128, 64, 1), ones_rowf[0:1, :], "c_onesrf")
    ld(FA(o_rsig, 1), rsig_bc[:, :], "c_rsig")

    def row_bf(src, off, p0, key):
        p.dma("sy", lambda e, s=src, p0=p0: e.dma_start(
            out=FA(o_xb[0], 1024, p0, 1), in_=s),
            reads=(), writes=("xb0",))
        p.op("ve", lambda e, off=off, p0=p0: e.tensor_copy(
            out=B_(off, 1024, p0, 1), in_=FA(o_xb[0], 1024, p0, 1)),
            reads=("xb0",), writes=(key,))

    row_bf(bqkv_vrow[0:1, :], o_rows, 0, "c_bqv")
    row_bf(bv_row[0:1, :], o_rows, 32, "c_bv")
    row_bf(bo_row[0:1, :], o_rows, 64, "c_bo")
    row_bf(bproj_row[0:1, :], o_rows2, 0, "c_bp")
    p.dma("sy", lambda e: e.dma_start(out=FA(o_xb[1], 8, 0, 1),
                                      in_=br_row[0:1, :]),
          reads=(), writes=("xb1",))
    p.op("ve", lambda e: e.tensor_copy(out=B_(o_brrow, 8, 0, 1),
                                       in_=FA(o_xb[1], 8, 0, 1)),
         reads=("xb1",), writes=("c_br",))
    p.dma("sy", lambda e: e.dma_start(out=FB(0, 1024, 0, 8), in_=eb2[:, :]),
          reads=(), writes=("eb2f",))
    p.op("ve", lambda e: e.tensor_copy(out=B_(o_eb2, 1024, 0, 8),
                                       in_=FB(0, 1024, 0, 8)),
         reads=("eb2f",), writes=("c_eb2",))
    p.dma("sy", lambda e: e.dma_start(
        out=FB(1024, 64).rearrange("q (dt e) -> q dt e", e=8),
        in_=Wr[:, :].rearrange("(dt q) e -> q dt e", q=128)),
        reads=(), writes=("wrf",))
    p.op("ve", lambda e: e.tensor_copy(out=B_(o_Wr, 64), in_=FB(1024, 64)),
         reads=("wrf",), writes=("c_Wr",))
    p.op("ve", lambda e: e.tensor_scalar_mul(out=FA(o_rsig2, 1),
                                             in0=FA(o_rsig, 1), scalar1=2.0),
         reads=("c_rsig",), writes=("c_rsig2",))
    p.op("ve", lambda e: e.tensor_scalar_mul(out=FA(o_nrsig, 1),
                                             in0=FA(o_rsig, 1), scalar1=-1.0),
         reads=("c_rsig",), writes=("c_nrsig",))

    WK = tuple(f"W{d}" for d in range(8))
    OWNK = tuple(f"own{d}" for d in range(8))
    XNT = tuple(f"xnT{d}" for d in range(8))
    HT = tuple(f"hT{d}" for d in range(8))
    TSK = tuple(f"ts{i}_{d}" for i in range(2) for d in range(8))

    def ln_chain(xkey, xoff, outoff, outkey):
        r = o_red
        p.op("ve", lambda e: e.reduce_sum(out=FA(r, 1), in_=FA(xoff, 1024),
                                          axis=AX.X, negate=True),
             reads=(xkey,), writes=("red0",))
        p.op("ve", lambda e: e.tensor_scalar_mul(out=FA(r, 1), in0=FA(r, 1),
                                                 scalar1=1.0 / 1024),
             reads=("red0",), writes=("red0",))
        p.op("ve", lambda e: e.tensor_scalar_add(out=FA(o_xc, 1024),
                                                 in0=FA(xoff, 1024),
                                                 scalar1=FA(r, 1)),
             reads=(xkey, "red0"), writes=("xc",))
        p.op("se", lambda e: e.activation(out=FA(o_sq, 1024),
                                          in_=FA(o_xc, 1024),
                                          func=AF.Square),
             reads=("xc",), writes=("sq",))
        p.op("ve", lambda e: e.reduce_sum(out=FA(r + 1, 1),
                                          in_=FA(o_sq, 1024), axis=AX.X),
             reads=("sq",), writes=("red1",))
        p.op("ve", lambda e: e.tensor_scalar(
            out=FA(r + 4, 1), in0=FA(r + 1, 1), scalar1=1.0 / 1024,
            scalar2=EPS, op0=ALU.mult, op1=ALU.add),
            reads=("red1",), writes=("red4",))
        p.op("se", lambda e: e.activation(out=FA(r + 5, 1), in_=FA(r + 4, 1),
                                          func=AF.Sqrt),
             reads=("red4",), writes=("red5",))
        p.op("ve", lambda e: e.reciprocal(out=FA(r + 2, 1), in_=FA(r + 5, 1)),
             reads=("red5",), writes=("red2",))
        p.op("ve", lambda e: e.tensor_scalar_mul(out=FA(outoff, 1024),
                                                 in0=FA(o_xc, 1024),
                                                 scalar1=FA(r + 2, 1)),
             reads=("xc", "red2"), writes=(outkey,))

    for it in range(32):
        xk = f"xb{it % 2}"
        p.dma("sy", lambda e, it=it: e.dma_start(
            out=FA(o_xb[it % 2], 1024), in_=x[it * 128:(it + 1) * 128, :]),
            reads=(), writes=(xk,))
        ln_chain(xk, o_xb[it % 2], o_xn1, "xn1")
        ln_chain("xn1", o_xn1, o_h1, "h1")
        for dt in range(8):
            p.op("te", lambda e, dt=dt: e.transpose(
                out=pb[6][0:128, 0:128],
                in_=afa[:, o_xn1 + dt * 128:o_xn1 + dt * 128 + 128],
                identity=FA(o_ident, 128)),
                reads=("xn1", "c_ident"), writes=("p6",))
            p.op("se", lambda e, dt=dt: e.copy(
                out=B_(o_tstage[0] + dt * 128, 128), in_=pb[6][0:128, 0:128]),
                reads=("p6",), writes=(f"ts0_{dt}",))
        for dt in range(8):
            p.dma("sy", lambda e, dt=dt, it=it: e.dma_start(
                out=s_xnT[dt, :, it * 128:(it + 1) * 128],
                in_=B_(o_tstage[0] + dt * 128, 128)),
                reads=(f"ts0_{dt}",), writes=())
        for dt in range(8):
            p.op("te", lambda e, dt=dt: e.transpose(
                out=pb[6][0:128, 0:128],
                in_=afa[:, o_h1 + dt * 128:o_h1 + dt * 128 + 128],
                identity=FA(o_ident, 128)),
                reads=("h1", "c_ident"), writes=("p6",))
            p.op("ve", lambda e, dt=dt: e.tensor_scalar(
                out=B_(o_tstage[1] + dt * 128, 128),
                in0=pb[6][0:128, 0:128],
                scalar1=afa[:, o_l1g + dt:o_l1g + dt + 1],
                scalar2=afa[:, o_l1b + dt:o_l1b + dt + 1],
                op0=ALU.mult, op1=ALU.add),
                reads=("p6", "c_l1g", "c_l1b"), writes=(f"ts1_{dt}",))
        for dt in range(8):
            p.dma("sy", lambda e, dt=dt, it=it: e.dma_start(
                out=s_hT[dt, :, it * 128:(it + 1) * 128],
                in_=B_(o_tstage[1] + dt * 128, 128)),
                reads=(f"ts1_{dt}",), writes=())

    def load_w_pass(Wsrc, col0, rdeps=()):
        for dt in range(8):
            wb = dt % 2
            p.dma("vq", lambda e, dt=dt, wb=wb: e.dma_start(
                out=FA(o_xb[wb], 1024),
                in_=Wsrc[dt * 128:(dt + 1) * 128, col0:col0 + 1024]),
                reads=(), writes=(f"xb{wb}",))
            p.op("ve", lambda e, dt=dt, wb=wb: e.tensor_copy(
                out=B_(o_wsb + dt * 1024, 1024), in_=FA(o_xb[wb], 1024)),
                reads=(f"xb{wb}",) + tuple(rdeps), writes=(f"W{dt}",))

    def ownc(dt, oc):
        ap5 = abf[:, o_big0:o_big0 + 32768].rearrange(
            "q (d m par z) -> q d par m z", d=8, m=16, par=2, z=128)
        return ap5[:, dt, 0, 4 * oc:4 * oc + 4, :]

    for dt in range(8):
        p.dma("sy", lambda e, dt=dt: e.dma_start(
            out=B_(o_big0 + dt * 4096, 4096), in_=s_xnT[dt, :, :]),
            reads=(), writes=(f"xnT{dt}",))

    load_w_pass(Wqkv, 0)
    for nt in range(8):
        for ic in range(8):
            bank, bk = pb[ic % 2], PK[ic % 2]

            def mmg(e, nt=nt, ic=ic, bank=bank):
                for dt in range(8):
                    inst = e.matmul(
                        out=bank[0:128, 0:512],
                        lhsT=B_(o_wsb + dt * 1024 + nt * 128, 128),
                        rhs=B_(o_big0 + dt * 4096 + ic * 512, 512),
                        start=(dt == 0), stop=(dt == 7))
                return inst
            p.op("te", mmg, reads=XNT + WK, writes=(bk,))
            p.op("ve", lambda e, nt=nt, bank=bank: e.tensor_scalar_add(
                out=FA(o_xc, 512), in0=bank[0:128, 0:512],
                scalar1=afa[:, o_bqkvc + nt:o_bqkvc + nt + 1]),
                reads=(bk, "c_bqkvc"), writes=("xc",))
            p.op("ve", lambda e: e.tensor_scalar_mul(
                out=B_(o_bfst[0], 512), in0=FA(o_xc, 512),
                scalar1=FA(o_rsig2, 1)),
                reads=("xc", "c_rsig2"), writes=("bfst0",))
            p.op("se", lambda e: e.activation(out=FA(o_sq, 512),
                                              in_=FA(o_xc, 512),
                                              func=AF.Square),
                 reads=("xc",), writes=("sq",))
            for hh in range(2):
                h = 2 * nt + hh
                p.op("te", lambda e, hh=hh: e.matmul(
                    out=pb[7][0:1, 0:512],
                    lhsT=afa[hh * 64:hh * 64 + 64, o_onescf:o_onescf + 1],
                    rhs=afa[hh * 64:hh * 64 + 64, o_sq:o_sq + 512],
                    start=True, stop=True),
                    reads=("sq", "c_onescf"), writes=("p7",))
                p.op("ve", lambda e: e.tensor_scalar_mul(
                    out=B_(o_bfst[1], 512, 0, 1), in0=pb[7][0:1, 0:512],
                    scalar1=afa[0:1, o_nrsig:o_nrsig + 1]),
                    reads=("p7", "c_nrsig"), writes=("bfst1",))
                p.dma("sy", lambda e, h=h, ic=ic: e.dma_start(
                    out=s_qr[h, 65:66, ic * 512:(ic + 1) * 512],
                    in_=B_(o_bfst[1], 512, 0, 1)),
                    reads=("bfst1",), writes=())
                p.dma("sy", lambda e, h=h, hh=hh, ic=ic: e.dma_start(
                    out=s_qr[h, 0:64, ic * 512:(ic + 1) * 512],
                    in_=B_(o_bfst[0], 512, hh * 64, 64)),
                    reads=("bfst0",), writes=())
    for h in range(H):
        p.dma("sy", lambda e, h=h: e.dma_start(
            out=s_qr[h, 64:65, 0:2048], in_=ones_bf[0:1, :]),
            reads=(), writes=())
        p.dma("sy", lambda e, h=h: e.dma_start(
            out=s_qr[h, 64:65, 2048:4096], in_=ones_bf[0:1, :]),
            reads=(), writes=())

    load_w_pass(Wqkv, 1024)
    for oc in range(4):
        for dt in range(8):
            p.op("ve", lambda e, dt=dt, oc=oc: e.tensor_copy(
                out=B_(o_own + dt * 512, 512).rearrange(
                    "q (j z) -> q j z", z=128),
                in_=ownc(dt, oc)),
                reads=XNT + (TSK if oc == 0 else ()),
                writes=(f"own{dt}",))
        for nt in range(8):
            bank, bk = pb[nt % 2], PK[nt % 2]

            def mmk(e, nt=nt, bank=bank):
                for dt in range(8):
                    inst = e.matmul(
                        out=bank[0:128, 0:512],
                        lhsT=B_(o_wsb + dt * 1024 + nt * 128, 128),
                        rhs=B_(o_own + dt * 512, 512),
                        start=(dt == 0), stop=(dt == 7))
                return inst
            p.op("te", mmk, reads=OWNK + WK, writes=(bk,))
            p.op("ve", lambda e, nt=nt, bank=bank: e.tensor_scalar_add(
                out=FA(o_xc, 512), in0=bank[0:128, 0:512],
                scalar1=afa[:, o_bqkvc + 8 + nt:o_bqkvc + 9 + nt]),
                reads=(bk, "c_bqkvc"), writes=("xc",))
            p.op("ve", lambda e: e.tensor_copy(
                out=B_(o_bfst[0], 512), in_=FA(o_xc, 512)),
                reads=("xc",), writes=("bfst0",))
            p.op("se", lambda e: e.activation(out=FA(o_sq, 512),
                                              in_=FA(o_xc, 512),
                                              func=AF.Square),
                 reads=("xc",), writes=("sq",))
            for hh in range(2):
                h = 2 * nt + hh
                p.op("te", lambda e, hh=hh: e.matmul(
                    out=pb[7][0:1, 0:512],
                    lhsT=afa[hh * 64:hh * 64 + 64, o_onescf:o_onescf + 1],
                    rhs=afa[hh * 64:hh * 64 + 64, o_sq:o_sq + 512],
                    start=True, stop=True),
                    reads=("sq", "c_onescf"), writes=("p7",))
                p.op("ve", lambda e: e.tensor_scalar_mul(
                    out=B_(o_bfst[1], 512, 0, 1), in0=pb[7][0:1, 0:512],
                    scalar1=afa[0:1, o_nrsig:o_nrsig + 1]),
                    reads=("p7", "c_nrsig"), writes=("bfst1",))
                p.dma("sy", lambda e, h=h, oc=oc: e.dma_start(
                    out=s_kh[h, 64:65, oc * 512:(oc + 1) * 512],
                    in_=B_(o_bfst[1], 512, 0, 1)),
                    reads=("bfst1",), writes=())
                p.dma("sy", lambda e, h=h, hh=hh, oc=oc: e.dma_start(
                    out=s_kh[h, 0:64, oc * 512:(oc + 1) * 512],
                    in_=B_(o_bfst[0], 512, hh * 64, 64)),
                    reads=("bfst0",), writes=())
    for h in range(H):
        p.dma("sy", lambda e, h=h: e.dma_start(
            out=s_kh[h, 65:66, 0:2048], in_=ones_bf[0:1, :]),
            reads=(), writes=())

    load_w_pass(Wqkv, 2048)
    for it in range(32):
        for nc2 in range(2):
            bank, bk = pb[(it * 2 + nc2) % 2], PK[(it * 2 + nc2) % 2]

            def mmv(e, it=it, nc2=nc2, bank=bank):
                e.matmul(out=bank[0:128, 0:512],
                         lhsT=B_(o_ones128, 128, 0, 1),
                         rhs=B_(o_rows + nc2 * 512, 512, 0, 1),
                         start=True, stop=False)
                for dt in range(8):
                    inst = e.matmul(
                        out=bank[0:128, 0:512],
                        lhsT=B_(o_big0 + dt * 4096 + it * 128, 128),
                        rhs=B_(o_wsb + dt * 1024 + nc2 * 512, 512),
                        start=False, stop=(dt == 7))
                return inst
            p.op("te", mmv, reads=XNT + WK + ("c_ones128", "c_bqv"),
                 writes=(bk,))
            p.op("se", lambda e, bank=bank: e.copy(
                out=B_(o_bfst[0], 512), in_=bank[0:128, 0:512]),
                reads=(bk,), writes=("bfst0",))
            p.dma("sy", lambda e, it=it, nc2=nc2: e.dma_start(
                out=s_vr[nc2, it, :, :], in_=B_(o_bfst[0], 512)),
                reads=("bfst0",), writes=())

    for dt in range(8):
        p.dma("sy", lambda e, dt=dt: e.dma_start(
            out=B_(o_big0 + dt * 4096, 4096), in_=s_hT[dt, :, :]),
            reads=(), writes=(f"hT{dt}", f"xnT{dt}"))

    load_w_pass(Wk, 0)
    for nt in range(8):
        for ic in range(8):
            bank, bk = pb[ic % 2], PK[ic % 2]

            def mmck(e, nt=nt, ic=ic, bank=bank):
                for dt in range(8):
                    inst = e.matmul(
                        out=bank[0:128, 0:512],
                        lhsT=B_(o_wsb + dt * 1024 + nt * 128, 128),
                        rhs=B_(o_big0 + dt * 4096 + ic * 512, 512),
                        start=(dt == 0), stop=(dt == 7))
                return inst
            p.op("te", mmck, reads=HT + WK, writes=(bk,))
            p.op("ve", lambda e, nt=nt, bank=bank: e.tensor_scalar(
                out=B_(o_bfst[0], 512), in0=bank[0:128, 0:512],
                scalar1=afa[:, o_bkc + nt:o_bkc + nt + 1], scalar2=None,
                op0=ALU.add),
                reads=(bk, "c_bkc"), writes=("bfst0",))
            for hh in range(2):
                p.dma("sy", lambda e, nt=nt, hh=hh, ic=ic: e.dma_start(
                    out=s_k[2 * nt + hh, :, ic * 512:(ic + 1) * 512],
                    in_=B_(o_bfst[0], 512, hh * 64, 64)),
                    reads=("bfst0",), writes=())

    load_w_pass(Wq, 0)
    for oc in range(4):
        for dt in range(8):
            p.op("ve", lambda e, dt=dt, oc=oc: e.tensor_copy(
                out=B_(o_own + dt * 512, 512).rearrange(
                    "q (j z) -> q j z", z=128),
                in_=ownc(dt, oc)),
                reads=HT, writes=(f"own{dt}",))
        for nt in range(8):
            bank, bk = pb[nt % 2], PK[nt % 2]

            def mmcq(e, nt=nt, bank=bank):
                for dt in range(8):
                    inst = e.matmul(
                        out=bank[0:128, 0:512],
                        lhsT=B_(o_wsb + dt * 1024 + nt * 128, 128),
                        rhs=B_(o_own + dt * 512, 512),
                        start=(dt == 0), stop=(dt == 7))
                return inst
            p.op("te", mmcq, reads=OWNK + WK, writes=(bk,))
            p.op("ve", lambda e, nt=nt, bank=bank: e.tensor_scalar(
                out=B_(o_bfst[0], 512), in0=bank[0:128, 0:512],
                scalar1=afa[:, o_bqc + nt:o_bqc + nt + 1], scalar2=0.125,
                op0=ALU.add, op1=ALU.mult),
                reads=(bk, "c_bqc"), writes=("bfst0",))
            for hh in range(2):
                p.dma("sy", lambda e, nt=nt, hh=hh, oc=oc: e.dma_start(
                    out=s_q[2 * nt + hh, :, oc * 512:(oc + 1) * 512],
                    in_=B_(o_bfst[0], 512, hh * 64, 64)),
                    reads=("bfst0",), writes=())

    load_w_pass(Wv, 0)
    for it in range(32):
        for nc2 in range(2):
            bank, bk = pb[(it * 2 + nc2) % 2], PK[(it * 2 + nc2) % 2]

            def mmcv(e, it=it, nc2=nc2, bank=bank):
                e.matmul(out=bank[0:128, 0:512],
                         lhsT=B_(o_ones128, 128, 32, 1),
                         rhs=B_(o_rows + nc2 * 512, 512, 32, 1),
                         start=True, stop=False)
                for dt in range(8):
                    inst = e.matmul(
                        out=bank[0:128, 0:512],
                        lhsT=B_(o_big0 + dt * 4096 + it * 128, 128),
                        rhs=B_(o_wsb + dt * 1024 + nc2 * 512, 512),
                        start=False, stop=(dt == 7))
                return inst
            p.op("te", mmcv, reads=HT + WK + ("c_ones128", "c_bv"),
                 writes=(bk,))
            p.op("se", lambda e, bank=bank: e.copy(
                out=B_(o_bfst[0], 512), in_=bank[0:128, 0:512]),
                reads=(bk,), writes=("bfst0",))
            p.dma("sy", lambda e, it=it, nc2=nc2: e.dma_start(
                out=s_v[nc2, it, :, :], in_=B_(o_bfst[0], 512)),
                reads=("bfst0",), writes=())

    if DBG:
        dbg_outs = {
            "d_xn": nc.dram_tensor("d_xn", [128, T], BF16,
                                   kind="ExternalOutput").ap(),
            "d_h": nc.dram_tensor("d_h", [128, T], BF16,
                                  kind="ExternalOutput").ap(),
            "d_qr": nc.dram_tensor("d_qr", [66, T], BF16,
                                   kind="ExternalOutput").ap(),
            "d_kh": nc.dram_tensor("d_kh", [66, NOWN], BF16,
                                   kind="ExternalOutput").ap(),
            "d_k": nc.dram_tensor("d_k", [64, T], BF16,
                                  kind="ExternalOutput").ap(),
            "d_q": nc.dram_tensor("d_q", [64, NOWN], BF16,
                                  kind="ExternalOutput").ap(),
            "d_v": nc.dram_tensor("d_v", [32, 128, 512], BF16,
                                  kind="ExternalOutput").ap(),
            "d_oT": nc.dram_tensor("d_oT", [128, 16384], BF16,
                                   kind="ExternalOutput").ap(),
            "d_a": nc.dram_tensor("d_a", [NB, 128, D], F32,
                                  kind="ExternalOutput").ap(),
            "d_w": nc.dram_tensor("d_w", [128, 64], F32,
                                  kind="ExternalOutput").ap(),
            "d_part": nc.dram_tensor("d_part", [NB, 128, D], F32,
                                     kind="ExternalOutput").ap(),
            "d_orT": nc.dram_tensor("d_orT", [128, 16384], BF16,
                                    kind="ExternalOutput").ap(),
        }
        for nm, src in [("d_xn", s_xnT[0, :, :]), ("d_h", s_hT[0, :, :]),
                        ("d_qr", s_qr[0, :, :]), ("d_kh", s_kh[0, :, :]),
                        ("d_k", s_k[0, :, :]), ("d_q", s_q[0, :, :]),
                        ("d_v", s_v[0][:, :, 0:512])]:
            p.dma("sy", lambda e, d=dbg_outs[nm], s=src: e.dma_start(
                out=d[:, :] if len(d.shape) == 2 else d[:, :, :], in_=s),
                reads=(), writes=())

    p.op("ve", lambda e: e.memset(FA(o_red + 7, 1), 0.0),
         reads=(), writes=HT + XNT + OWNK + TSK + ("bfst0", "bfst1", "PH2"))

    p.dma("sy", lambda e: e.dma_start(
        out=B_(o_masks, 4096).rearrange("q (m n) -> q m n", n=512),
        in_=masks_in[:, :, :].rearrange("m q n -> q m n")),
        reads=("PH2",), writes=("c_masks",))
    for hb in range(2):
        p.op("ve", lambda e, hb=hb: e.memset(
            abf[:, o_vb[hb]:o_vb[hb] + 2080].rearrange(
                "q (j n) -> q j n", n=65)[:, :, 64:65], 1.0),
            reads=("PH2",), writes=(f"vones{hb}",))

    for h in range(H):
        hb = h % 2
        p.dma("sy", lambda e, h=h, hb=hb: e.dma_start(
            out=B_(o_kb[hb], 4096, 0, 64), in_=s_k[h, :, :]),
            reads=("PH2",), writes=(f"kb{hb}",))
        p.dma("sy", lambda e, h=h, hb=hb: e.dma_start(
            out=abf[:, o_vb[hb]:o_vb[hb] + 2080].rearrange(
                "q (j n) -> q j n", n=65)[:, :, 0:64],
            in_=s_v[h // 8][:, :, (h % 8) * 64:(h % 8) * 64 + 64].rearrange(
                "j q n -> q j n")),
            reads=(f"vones{hb}",), writes=(f"vb{hb}",))
        for oc in range(4):
            qb = (h * 4 + oc) % 2
            p.dma("sy", lambda e, h=h, oc=oc, qb=qb: e.dma_start(
                out=B_(o_qs[qb], 512, 0, 64),
                in_=s_q[h, :, oc * 512:(oc + 1) * 512]),
                reads=("PH2",), writes=(f"qs{qb}",))
            pvbank, pvk = pb[4 + (h * 4 + oc) % 2], PK[4 + (h * 4 + oc) % 2]
            nj = 8 * oc + 8
            for Jp in range(nj // 2):
                J0 = 2 * Jp

                def mmS(e, J0=J0, hb=hb, qb=qb):
                    for dj in range(2):
                        inst = e.matmul(
                            out=pb23[0:128, dj * 512:dj * 512 + 512],
                            lhsT=B_(o_kb[hb] + (J0 + dj) * 128, 128, 0, 64),
                            rhs=B_(o_qs[qb], 512, 0, 64),
                            start=True, stop=True)
                    return inst
                p.op("te", mmS, reads=(f"kb{hb}", f"qs{qb}"),
                     writes=("p2", "p3"))
                if J0 + 2 <= 8 * oc:
                    p.op("se", lambda e: e.activation(
                        out=B_(o_pbuf[0], 1024), in_=pb23[0:128, 0:1024],
                        func=AF.Exp),
                        reads=("p2", "p3", "PH2"), writes=("pbf",))
                else:
                    r = J0 - 8 * oc
                    p.op("se", lambda e: e.activation(
                        out=FA(o_xc, 1024), in_=pb23[0:128, 0:1024],
                        func=AF.Exp),
                        reads=("p2", "p3"), writes=("xc",))
                    p.op("ve", lambda e, r=r: e.tensor_tensor(
                        out=B_(o_pbuf[0], 1024), in0=FA(o_xc, 1024),
                        in1=B_(o_masks + r * 512, 1024), op=ALU.mult),
                        reads=("xc", "c_masks", "PH2"), writes=("pbf",))

                def mmPV(e, J0=J0, hb=hb, nj=nj, pvbank=pvbank):
                    for dj in range(2):
                        J = J0 + dj
                        inst = e.matmul(
                            out=pvbank[0:65, 0:512],
                            lhsT=abf[:, o_vb[hb] + J * 65:
                                     o_vb[hb] + J * 65 + 65],
                            rhs=B_(o_pbuf[0] + dj * 512, 512),
                            start=(J == 0), stop=(J == nj - 1))
                    return inst
                p.op("te", mmPV,
                     reads=(f"vb{hb}", f"vones{hb}", "pbf"), writes=(pvk,))
            p.op("ve", lambda e, pvbank=pvbank: e.reciprocal(
                out=FA(o_xc, 512, 64, 1), in_=pvbank[64:65, 0:512]),
                reads=(pvk,), writes=("xc",))
            p.op("te", lambda e: e.matmul(
                out=pb[6][0:64, 0:512],
                lhsT=afa[64:65, o_onesrf:o_onesrf + 64],
                rhs=FA(o_xc, 512, 64, 1), start=True, stop=True),
                reads=("xc", "c_onesrf"), writes=("p6",))
            p.op("se", lambda e: e.copy(out=FA(o_xb[0], 512, 0, 64),
                                        in_=pb[6][0:64, 0:512]),
                 reads=("p6",), writes=("xb0",))
            p.op("ve", lambda e, pvbank=pvbank: e.tensor_tensor(
                out=B_(o_bfst[0], 512, 0, 64),
                in0=pvbank[0:64, 0:512], in1=FA(o_xb[0], 512, 0, 64),
                op=ALU.mult),
                reads=(pvk, "xb0", "PH2"), writes=("bfst0",))
            p.dma("sy", lambda e, h=h, oc=oc: e.dma_start(
                out=abf[64 * (h % 2):64 * (h % 2) + 64,
                        o_oT + (h // 2) * 2048 + oc * 512:
                        o_oT + (h // 2) * 2048 + oc * 512 + 512],
                in_=B_(o_bfst[0], 512, 0, 64)),
                reads=("bfst0", "PH2"), writes=(f"oT{h}_{oc}",))
    OT = tuple(f"oT{h}_{oc}" for h in range(H) for oc in range(4))

    if DBG:
        p.dma("sy", lambda e: e.dma_start(
            out=dbg_outs["d_oT"][:, :], in_=abf[:, o_oT:o_oT + 16384]),
            reads=OT, writes=())

    load_w_pass(Wo, 0)
    for it in range(NB):
        r0 = 256 * it
        xk = f"xb{it % 2}"
        p.dma("sy", lambda e, r0=r0, it=it: e.dma_start(
            out=FA(o_xb[it % 2], 1024), in_=x[r0:r0 + 128, :]),
            reads=(), writes=(xk,))
        ln_chain(xk, o_xb[it % 2], o_xn1, "xn1")
        for nc2 in range(2):
            bank, bk = pb[nc2], PK[nc2]

            def mma(e, it=it, nc2=nc2, bank=bank):
                e.matmul(out=bank[0:128, 0:512],
                         lhsT=B_(o_ones128, 128, 64, 1),
                         rhs=B_(o_rows + nc2 * 512, 512, 64, 1),
                         start=True, stop=False)
                for dt in range(8):
                    inst = e.matmul(
                        out=bank[0:128, 0:512],
                        lhsT=B_(o_oT + dt * 2048 + it * 128, 128),
                        rhs=B_(o_wsb + dt * 1024 + nc2 * 512, 512),
                        start=False, stop=(dt == 7))
                return inst
            p.op("te", mma, reads=OT + WK + ("c_ones128", "c_bo"),
                 writes=(bk,))
            p.op("ve", lambda e, nc2=nc2, bank=bank: e.tensor_tensor(
                out=afa[:, o_xc + nc2 * 512:o_xc + nc2 * 512 + 512],
                in0=bank[0:128, 0:512],
                in1=afa[:, o_xn1 + nc2 * 512:o_xn1 + nc2 * 512 + 512],
                op=ALU.add),
                reads=(bk, "xn1"), writes=("xc",))
        p.dma("sy", lambda e, it=it: e.dma_start(
            out=s_a[it, :, :], in_=FA(o_xc, 1024)),
            reads=("xc",), writes=(f"sa{it}",))
        if DBG:
            p.dma("sy", lambda e, it=it: e.dma_start(
                out=dbg_outs["d_a"][it, :, :], in_=FA(o_xc, 1024)),
                reads=("xc",), writes=())

    MIDK = tuple(f"mid{f}_{c}" for f in range(32) for c in range(2))
    H2K = tuple(f"h2T{d}_{i}" for d in range(8) for i in range(8))
    W2K = tuple(f"w2b{f}" for f in range(32))

    p.op("ve", lambda e: e.memset(FA(o_red + 7, 1), 0.0),
         reads=(), writes=OT + WK +
         ("kb0", "kb1", "vb0", "vb1", "vones0", "vones1", "qs0", "qs1",
          "pbf0", "pbf1", "c_masks", "eb2f", "wrf", "PH3"))

    for half in range(2):
        for itl in range(8):
            it = half * 8 + itl
            xk = f"xb{itl % 2}"
            p.dma("sy", lambda e, it=it, itl=itl: e.dma_start(
                out=FA(o_xb[itl % 2], 1024), in_=s_a[it, :, :]),
                reads=(f"sa{it}",), writes=(xk,))
            ln_chain(xk, o_xb[itl % 2], o_xn1, "xn1")
            for dt in range(8):
                p.op("te", lambda e, dt=dt: e.transpose(
                    out=pb[6][0:128, 0:128],
                    in_=afa[:, o_xn1 + dt * 128:o_xn1 + dt * 128 + 128],
                    identity=FA(o_ident, 128)),
                    reads=("xn1", "c_ident"), writes=("p6",))
                p.op("ve", lambda e, dt=dt, itl=itl: e.tensor_scalar(
                    out=B_(o_h2T + dt * 1024 + itl * 128, 128),
                    in0=pb[6][0:128, 0:128],
                    scalar1=afa[:, o_l2g + dt:o_l2g + dt + 1],
                    scalar2=afa[:, o_l2b + dt:o_l2b + dt + 1],
                    op0=ALU.mult, op1=ALU.add),
                    reads=("p6", "c_l2g", "c_l2b", "PH3"),
                    writes=(f"h2T{dt}_{itl}",))

            def mmr(e, itl=itl):
                e.matmul(out=pb[6][0:128, 0:8],
                         lhsT=B_(o_ones128, 128, 0, 1),
                         rhs=B_(o_brrow, 8, 0, 1), start=True, stop=False)
                for dt in range(8):
                    inst = e.matmul(
                        out=pb[6][0:128, 0:8],
                        lhsT=B_(o_h2T + dt * 1024 + itl * 128, 128),
                        rhs=B_(o_Wr + dt * 8, 8),
                        start=False, stop=(dt == 7))
                return inst
            p.op("te", mmr, reads=("c_ones128", "c_br", "c_Wr") +
                 tuple(f"h2T{d}_{itl}" for d in range(8)), writes=("p6",))
            r = o_red
            p.op("ve", lambda e: e.reduce_max(
                out=FA(r, 1), in_=pb[6][0:128, 0:8], axis=AX.X, negate=True),
                reads=("p6",), writes=("red0",))
            p.op("ve", lambda e: e.tensor_scalar(
                out=FA(o_sq, 8), in0=pb[6][0:128, 0:8], scalar1=FA(r, 1),
                scalar2=0.0, op0=ALU.add, op1=ALU.is_ge),
                reads=("p6", "red0"), writes=("sq",))
            p.op("ve", lambda e: e.scalar_tensor_tensor(
                out=FA(o_sq + 16, 8), in0=FA(o_sq, 8), scalar=-1e30,
                in1=pb[6][0:128, 0:8], op0=ALU.mult, op1=ALU.add),
                reads=("sq", "p6"), writes=("lm",))
            p.op("ve", lambda e: e.reduce_max(
                out=FA(r + 1, 1), in_=FA(o_sq + 16, 8), axis=AX.X,
                negate=True),
                reads=("lm",), writes=("red1",))
            p.op("se", lambda e: e.activation(
                out=FA(o_sq + 32, 8), in_=pb[6][0:128, 0:8], func=AF.Exp,
                bias=FA(r, 1)),
                reads=("p6", "red0"), writes=("rexp",))
            p.op("ve", lambda e: e.tensor_scalar(
                out=FA(o_sq + 48, 8), in0=pb[6][0:128, 0:8],
                scalar1=FA(r + 1, 1), scalar2=0.0, op0=ALU.add,
                op1=ALU.is_ge),
                reads=("p6", "red1"), writes=("km",))
            p.op("ve", lambda e: e.tensor_tensor(
                out=FA(o_sq + 32, 8), in0=FA(o_sq + 32, 8),
                in1=FA(o_sq + 48, 8), op=ALU.mult),
                reads=("rexp", "km"), writes=("rexp",))
            p.op("ve", lambda e: e.reduce_sum(
                out=FA(r + 2, 1), in_=FA(o_sq + 32, 8), axis=AX.X),
                reads=("rexp",), writes=("red2",))
            p.op("ve", lambda e: e.reciprocal(out=FA(r + 3, 1),
                                              in_=FA(r + 2, 1)),
                 reads=("red2",), writes=("red3",))
            p.op("ve", lambda e, itl=itl: e.tensor_scalar_mul(
                out=afa[:, o_wrt + itl * 8:o_wrt + itl * 8 + 8],
                in0=FA(o_sq + 32, 8), scalar1=FA(r + 3, 1)),
                reads=("rexp", "red3"), writes=(f"wrt{itl}",))
            p.op("te", lambda e, itl=itl: e.transpose(
                out=pb[6][0:8, 0:128],
                in_=afa[:, o_wrt + itl * 8:o_wrt + itl * 8 + 8],
                identity=FA(o_ident, 128)),
                reads=(f"wrt{itl}", "c_ident"), writes=("p6",))
            p.op("se", lambda e, itl=itl: e.copy(
                out=B_(o_wT + itl * 128, 128, 0, 8), in_=pb[6][0:8, 0:128]),
                reads=("p6", "PH3"), writes=(f"wTt{itl}",))
        for itl in range(8):
            for dch in range(2):
                p.op("te", lambda e, itl=itl, dch=dch: e.matmul(
                    out=pb[6][0:128, 0:512],
                    lhsT=B_(o_wT + itl * 128, 128, 0, 8),
                    rhs=B_(o_eb2 + dch * 512, 512, 0, 8),
                    start=True, stop=True),
                    reads=(f"wTt{itl}", "c_eb2"), writes=("p6",))
                p.op("ve", lambda e, itl=itl, dch=dch: e.tensor_copy(
                    out=FB(itl * 1024 + dch * 512, 512),
                    in_=pb[6][0:128, 0:512]),
                    reads=("p6", "PH3"), writes=(f"acc{itl}_{dch}",))
        SW1 = [o_xb[0], o_xb[1], o_sq, o_xc]
        SW1K = ["xb0", "xb1", "sq", "xc"]
        SW2 = [o_xn1, o_xn1 + 512, o_h1, o_h1 + 512]
        SW2K = ["w2s0", "w2s1", "w2s2", "w2s3"]
        for ee in range(E):
            for ft in range(32):
                wb = ft % 2
                sb1 = ft % 4
                p.dma("vq", lambda e, ee=ee, ft=ft, sb1=sb1: e.dma_start(
                    out=FA(SW1[sb1], 1024).rearrange(
                        "q (dt f) -> q dt f", f=128),
                    in_=ew1[ee, :, ft * 128:(ft + 1) * 128].rearrange(
                        "(dt q) f -> q dt f", q=128)),
                    reads=(), writes=(SW1K[sb1],))
                p.op("ve", lambda e, wb=wb, sb1=sb1: e.tensor_copy(
                    out=B_(o_w1b[wb], 1024), in_=FA(SW1[sb1], 1024)),
                    reads=(SW1K[sb1],), writes=(f"w1b{wb}",))
                for ch in range(2):
                    bank, bk = pb[ch], PK[ch]

                    def mm1(e, ft=ft, ch=ch, wb=wb, bank=bank):
                        for dt in range(8):
                            inst = e.matmul(
                                out=bank[0:128, 0:512],
                                lhsT=B_(o_w1b[wb] + dt * 128, 128),
                                rhs=B_(o_h2T + dt * 1024 + ch * 512, 512),
                                start=(dt == 0), stop=(dt == 7))
                        return inst
                    p.op("te", mm1, reads=(f"w1b{wb}",) + H2K, writes=(bk,))
                    p.op("se", lambda e, ft=ft, ch=ch, ee=ee, bank=bank:
                         e.activation(
                             out=B_(o_mid + ft * 1024 + ch * 512, 512),
                             in_=bank[0:128, 0:512], func=AF.Gelu,
                             bias=afa[:, o_b1c + ee * 32 + ft:
                                      o_b1c + ee * 32 + ft + 1]),
                         reads=(bk, "c_b1c", "PH3"),
                         writes=(f"mid{ft}_{ch}",))
                p.dma("vq", lambda e, ee=ee, ft=ft: e.dma_start(
                    out=FA(SW2[ft % 4], 512),
                    in_=ew2[ee, ft * 128:(ft + 1) * 128, 0:512]),
                    reads=(), writes=(SW2K[ft % 4],) +
                    (("xn1", "h1") if ft == 0 else ()))
                p.op("ve", lambda e, ft=ft: e.tensor_copy(
                    out=B_(o_w2b + ft * 512, 512),
                    in_=FA(SW2[ft % 4], 512)),
                    reads=(SW2K[ft % 4], "PH3"), writes=(f"w2b{ft}",))
            for dch in range(2):
                if dch == 1:
                    for ft in range(32):
                        p.dma("vq", lambda e, ee=ee, ft=ft: e.dma_start(
                            out=FA(SW2[ft % 4], 512),
                            in_=ew2[ee, ft * 128:(ft + 1) * 128, 512:1024]),
                            reads=(), writes=(SW2K[ft % 4],))
                        p.op("ve", lambda e, ft=ft: e.tensor_copy(
                            out=B_(o_w2b + ft * 512, 512),
                            in_=FA(SW2[ft % 4], 512)),
                            reads=(SW2K[ft % 4],), writes=(f"w2b{ft}",))
                for itl in range(8):
                    bank, bk = pb[2 + itl % 2], PK[2 + itl % 2]

                    def mm2(e, itl=itl, bank=bank):
                        for ft in range(32):
                            inst = e.matmul(
                                out=bank[0:128, 0:512],
                                lhsT=B_(o_mid + ft * 1024 + itl * 128, 128),
                                rhs=B_(o_w2b + ft * 512, 512),
                                start=(ft == 0), stop=(ft == 31))
                        return inst
                    p.op("te", mm2, reads=MIDK + W2K, writes=(bk,))
                    p.op("ve", lambda e, itl=itl, dch=dch, ee=ee, bank=bank:
                         e.scalar_tensor_tensor(
                             out=FB(itl * 1024 + dch * 512, 512),
                             in0=bank[0:128, 0:512],
                             scalar=afa[:, o_wrt + itl * 8 + ee:
                                        o_wrt + itl * 8 + ee + 1],
                             in1=FB(itl * 1024 + dch * 512, 512),
                             op0=ALU.mult, op1=ALU.add),
                         reads=(bk, f"wrt{itl}", f"acc{itl}_{dch}"),
                         writes=(f"acc{itl}_{dch}",))
        for itl in range(8):
            it = half * 8 + itl
            r0 = 256 * it
            p.dma("sy", lambda e, it=it: e.dma_start(
                out=FA(o_xb[0], 1024), in_=s_a[it, :, :]),
                reads=(f"sa{it}",), writes=("xb0",))
            p.dma("sy", lambda e, r0=r0: e.dma_start(
                out=FA(o_xb[1], 1024), in_=x[r0:r0 + 128, :]),
                reads=(), writes=("xb1",))
            p.op("ve", lambda e, itl=itl: e.tensor_tensor(
                out=FA(o_xc, 1024), in0=FB(itl * 1024, 1024),
                in1=FA(o_xb[0], 1024), op=ALU.add),
                reads=(f"acc{itl}_0", f"acc{itl}_1", "xb0"), writes=("xc",))
            p.op("ve", lambda e: e.tensor_tensor(
                out=FA(o_xc, 1024), in0=FA(o_xc, 1024),
                in1=FA(o_xb[1], 1024), op=ALU.add),
                reads=("xc", "xb1"), writes=("xc",))
            p.dma("sy", lambda e, it=it: e.dma_start(
                out=s_a[it, :, :], in_=FA(o_xc, 1024)),
                reads=("xc",), writes=(f"sa{it}",))
            if DBG:
                p.dma("sy", lambda e, it=it: e.dma_start(
                    out=dbg_outs["d_part"][it, :, :], in_=FA(o_xc, 1024)),
                    reads=("xc",), writes=(f"dp{it}",))
        if DBG and half == 1:
            p.dma("sy", lambda e: e.dma_start(
                out=dbg_outs["d_w"][:, :], in_=FA(o_wrt, 64)),
                reads=tuple(f"wrt{i}" for i in range(8)), writes=())

    p.op("ve", lambda e: e.memset(FA(o_red + 7, 1), 0.0),
         reads=(), writes=MIDK + H2K + W2K +
         tuple(f"w1b{i}" for i in range(2)) +
         tuple(f"wTt{i}" for i in range(8)) + ("PH4",))

    load_w_pass(Wproj, 0, rdeps=("PH4",))
    for h in range(H):
        hb = h % 2
        p.dma("sy", lambda e, h=h, hb=hb: e.dma_start(
            out=abf[0:66, o_kh_b[hb]:o_kh_b[hb] + 2048], in_=s_kh[h, :, :]),
            reads=("PH4",), writes=(f"khb{hb}",))
        p.dma("sy", lambda e, h=h, hb=hb: e.dma_start(
            out=abf[0:66, o_qh_b[hb]:o_qh_b[hb] + 4096], in_=s_qr[h, :, :]),
            reads=("PH4",), writes=(f"qhb{hb}",))
        p.dma("sy", lambda e, h=h, hb=hb: e.dma_start(
            out=abf[:, o_vrb[hb]:o_vrb[hb] + 2048].rearrange(
                "q (j n) -> q j n", n=64),
            in_=s_vr[h // 8][:, :, (h % 8) * 64:(h % 8) * 64 + 64].rearrange(
                "j q n -> q j n")),
            reads=("PH4",), writes=(f"vrb{hb}",))
        for oc in range(4):
            pvbank, pvk = pb[4 + oc % 2], PK[4 + oc % 2]
            for Jp in range(16):
                J0 = 2 * Jp

                def mmD(e, J0=J0, hb=hb, oc=oc):
                    for dj in range(2):
                        inst = e.matmul(
                            out=pb23[0:128, dj * 512:dj * 512 + 512],
                            lhsT=abf[0:66, o_qh_b[hb] + (J0 + dj) * 128:
                                     o_qh_b[hb] + (J0 + dj) * 128 + 128],
                            rhs=abf[0:66, o_kh_b[hb] + oc * 512:
                                    o_kh_b[hb] + oc * 512 + 512],
                            start=True, stop=True)
                    return inst
                p.op("te", mmD, reads=(f"qhb{hb}", f"khb{hb}"),
                     writes=("p2", "p3"))
                p.op("se", lambda e: e.activation(
                    out=B_(o_eb[0], 1024), in_=pb23[0:128, 0:1024],
                    func=AF.Exp),
                    reads=("p2", "p3", "PH4"), writes=("ebf",))

                def mmPVr(e, J0=J0, hb=hb, pvbank=pvbank):
                    for dj in range(2):
                        J = J0 + dj
                        inst = e.matmul(
                            out=pvbank[0:64, 0:512],
                            lhsT=abf[:, o_vrb[hb] + J * 64:
                                     o_vrb[hb] + J * 64 + 64],
                            rhs=B_(o_eb[0] + dj * 512, 512),
                            start=(J == 0), stop=(J == 31))
                    return inst
                p.op("te", mmPVr, reads=(f"vrb{hb}", "ebf"), writes=(pvk,))
            p.op("se", lambda e, pvbank=pvbank: e.copy(
                out=B_(o_bfst[0], 512, 0, 64), in_=pvbank[0:64, 0:512]),
                reads=(pvk, "PH4"), writes=("bfst0",))
            p.dma("sy", lambda e, h=h, oc=oc: e.dma_start(
                out=abf[64 * (h % 2):64 * (h % 2) + 64,
                        o_orT + (h // 2) * 2048 + oc * 512:
                        o_orT + (h // 2) * 2048 + oc * 512 + 512],
                in_=B_(o_bfst[0], 512, 0, 64)),
                reads=("bfst0", "PH4"), writes=(f"orT{h}_{oc}",))
    ORT = tuple(f"orT{h}_{oc}" for h in range(H) for oc in range(4))

    if DBG:
        p.dma("sy", lambda e: e.dma_start(
            out=dbg_outs["d_orT"][:, :], in_=abf[:, o_orT:o_orT + 16384]),
            reads=ORT, writes=())

    for it in range(NB):
        xk = f"xb{it % 2}"
        p.dma("sy", lambda e, it=it: e.dma_start(
            out=FA(o_xb[it % 2], 1024), in_=s_a[it, :, :]),
            reads=(f"sa{it}",), writes=(xk,))
        for nc2 in range(2):
            bank, bk = pb[nc2], PK[nc2]

            def mmf(e, it=it, nc2=nc2, bank=bank):
                e.matmul(out=bank[0:128, 0:512],
                         lhsT=B_(o_ones128, 128, 0, 1),
                         rhs=B_(o_rows2 + nc2 * 512, 512, 0, 1),
                         start=True, stop=False)
                for dt in range(8):
                    inst = e.matmul(
                        out=bank[0:128, 0:512],
                        lhsT=B_(o_orT + dt * 2048 + it * 128, 128),
                        rhs=B_(o_wsb + dt * 1024 + nc2 * 512, 512),
                        start=False, stop=(dt == 7))
                return inst
            p.op("te", mmf, reads=ORT + WK + ("c_ones128", "c_bp"),
                 writes=(bk,))
            p.op("ve", lambda e, it=it, nc2=nc2, bank=bank: e.tensor_tensor(
                out=afa[:, o_xc + nc2 * 512:o_xc + nc2 * 512 + 512],
                in0=bank[0:128, 0:512],
                in1=afa[:, o_xb[it % 2] + nc2 * 512:
                        o_xb[it % 2] + nc2 * 512 + 512],
                op=ALU.add),
                reads=(bk, xk), writes=("xc",))
        p.dma("sy", lambda e, it=it: e.dma_start(
            out=out[it * 128:(it + 1) * 128, :], in_=FA(o_xc, 1024)),
            reads=("xc",), writes=())

    p.finish()
    p.emit(nc, sems)
    return st


def kernel(**inputs):
    gi = {k: np.asarray(v) for k, v in inputs.items()}
    x_full = np.ascontiguousarray(gi["L0_attn_x"], dtype=np.float32)

    def cols(v, k):
        return np.ascontiguousarray(
            np.asarray(v, np.float32).reshape(k, 128).T)

    rsig = np.asarray(gi["L0_attn_r_sigma"], np.float32).reshape(1, 1)
    common = {
        "Wqkv": np.ascontiguousarray(gi["L0_attn_Wqkv"], np.float32),
        "Wq": np.ascontiguousarray(gi["L0_ffn_Wq"], np.float32),
        "Wk": np.ascontiguousarray(gi["L0_ffn_Wk"], np.float32),
        "Wv": np.ascontiguousarray(gi["L0_ffn_Wv"], np.float32),
        "Wo": np.ascontiguousarray(gi["L0_ffn_Wo"], np.float32),
        "Wproj": np.ascontiguousarray(gi["L0_attn_Wproj"], np.float32),
        "Wr": np.ascontiguousarray(gi["L0_ffn_Wr"], np.float32),
        "ew1": np.ascontiguousarray(gi["L0_ffn_e_w1"], np.float32),
        "ew2": np.ascontiguousarray(gi["L0_ffn_e_w2"], np.float32),
        "eb2": np.ascontiguousarray(gi["L0_ffn_e_b2"], np.float32),
        "bqkv_c": cols(gi["L0_attn_bqkv"], 24),
        "bq_c": cols(gi["L0_ffn_bq"], 8),
        "bk_c": cols(gi["L0_ffn_bk"], 8),
        "l1g_c": cols(gi["L0_ffn_ln1_g"], 8),
        "l1b_c": cols(gi["L0_ffn_ln1_b"], 8),
        "l2g_c": cols(gi["L0_ffn_ln2_g"], 8),
        "l2b_c": cols(gi["L0_ffn_ln2_b"], 8),
        "b1cols": np.ascontiguousarray(
            np.asarray(gi["L0_ffn_e_b1"], np.float32).reshape(8, 32, 128)
            .transpose(2, 0, 1).reshape(128, 256)),
        "bqkv_vrow": np.ascontiguousarray(
            np.asarray(gi["L0_attn_bqkv"], np.float32)[2048:3072]
            .reshape(1, 1024)),
        "bv_row": np.ascontiguousarray(
            np.asarray(gi["L0_ffn_bv"], np.float32).reshape(1, 1024)),
        "bo_row": np.ascontiguousarray(
            np.asarray(gi["L0_ffn_bo"], np.float32).reshape(1, 1024)),
        "bproj_row": np.ascontiguousarray(
            np.asarray(gi["L0_attn_bproj"], np.float32).reshape(1, 1024)),
        "br_row": np.ascontiguousarray(
            np.asarray(gi["L0_ffn_br"], np.float32).reshape(1, 8)),
        "ident": np.eye(128, dtype=np.float32),
        "ones_colf": np.ones((128, 1), np.float32),
        "ones_rowf": np.ones((1, 128), np.float32),
        "ones_bf": np.ones((1, 2048), ml_dtypes.bfloat16),
        "ones128_bf": np.ones((1, 128), ml_dtypes.bfloat16),
        "rsig_bc": np.ascontiguousarray(np.broadcast_to(rsig, (128, 1))),
    }

    def mk_masks(parity):
        m = np.zeros((8, 128, 512), np.float32)
        tri = np.triu(np.ones((128, 128), np.float32))
        for r in range(8):
            for c in range(4):
                t = 2 * c
                if r < t or (parity == 1 and r == t + 1):
                    m[r, :, c * 128:(c + 1) * 128] = 1.0
                elif r == t:
                    m[r, :, c * 128:(c + 1) * 128] = tri
        return np.ascontiguousarray(m.astype(ml_dtypes.bfloat16))

    masks_by_par = [mk_masks(0), mk_masks(1)]

    in_maps = []
    for c in range(8):
        b, par = c // 2, c % 2
        d = dict(common)
        xb = x_full[b]
        if par == 1:
            xb = xb.reshape(16, 2, 128, 1024)[:, ::-1].reshape(4096, 1024)
        d["x"] = np.ascontiguousarray(xb)
        d["masks"] = masks_by_par[par]
        in_maps.append(d)

    nc = bass.Bass("TRN2", target_bir_lowering=False)
    stack = build_program(nc)
    res = run_bass_kernel_spmd(nc, in_maps, core_ids=list(range(8)))
    stack.close()

    outp = np.empty((4, 4096, 1024), np.float32)
    for c in range(8):
        b, par = c // 2, c % 2
        o = np.asarray(res.results[c]["out"]).reshape(16, 128, 1024)
        outp[b].reshape(16, 2, 128, 1024)[:, par] = o
    return outp


# version 04 (final)
# speedup vs baseline: 1.2983x; 1.2983x over previous
import numpy as np
import ml_dtypes
import concourse.bass as bass
import concourse.mybir as mybir
from concourse.bass_utils import run_bass_kernel_spmd
from contextlib import ExitStack

F32 = mybir.dt.float32
BF16 = mybir.dt.bfloat16
AF = mybir.ActivationFunctionType
ALU = mybir.AluOpType
AX = mybir.AxisListType

B, T, D, H, HD, E, FF = 4, 4096, 1024, 16, 64, 8, 4096
NOWN, NB = 2048, 16
EPS = 1e-5

ENGS = ("te", "ve", "se")
QUEUES = ("sy", "gp", "vq")
QENG = {"sy": "sy", "gp": "gp", "vq": "sy"}


class Prog:

    def __init__(self):
        self.items = {k: [] for k in ("te", "ve", "se", "gp", "sy")}
        self.ticks = {e: 0 for e in ENGS}
        self.dmas = {q: 0 for q in QUEUES}
        self.clock = {k: {} for k in ("te", "ve", "se", "gp", "sy")}
        self.state = {}

    def _need(self, reads, writes):
        evs = []
        for k in reads:
            st = self.state.get(k)
            if st and st["w"] is not None:
                evs.append(st["w"])
        for k in writes:
            st = self.state.get(k)
            if st:
                if st["w"] is not None:
                    evs.append(st["w"])
                evs.extend(st["r"])
        return evs

    def _wait(self, stream, evs):
        cl = self.clock[stream]
        for kind, src, val, evcl in evs:
            key = (kind, src)
            if kind == "t" and src == stream == "te":
                continue
            if cl.get(key, 0) >= val:
                continue
            self.items[stream].append(("wait", key, val))
            for kk, vv in evcl.items():
                if cl.get(kk, 0) < vv:
                    cl[kk] = vv
            if cl.get(key, 0) < val:
                cl[key] = val

    def op(self, eng, fn, reads=(), writes=()):
        evs = self._need(reads, writes)
        self._wait(eng, evs)
        self.ticks[eng] += 1
        val = self.ticks[eng]
        self.items[eng].append(("call", fn, ("t", eng), 1))
        evcl = dict(self.clock[eng])
        evcl[("t", eng)] = val
        ev = ("t", eng, val, evcl)
        for k in writes:
            self.state[k] = {"w": ev, "r": []}
        for k in reads:
            st = self.state.setdefault(k, {"w": None, "r": []})
            st["r"].append(ev)
        return ev

    def dma(self, q, fn, reads=(), writes=()):
        stream = QENG[q]
        evs = self._need(reads, writes)
        self._wait(stream, evs)
        self.dmas[q] += 1
        val = 16 * self.dmas[q]
        self.items[stream].append(("call", fn, ("d", q), 16))
        evcl = dict(self.clock[stream])
        evcl[("d", q)] = val
        ev = ("d", q, val, evcl)
        for k in writes:
            self.state[k] = {"w": ev, "r": []}
        for k in reads:
            st = self.state.setdefault(k, {"w": None, "r": []})
            st["r"].append(ev)
        return ev

    def finish(self):
        for q in QUEUES:
            if self.dmas[q]:
                self._wait(QENG[q], [("d", q, 16 * self.dmas[q], {})])
        for q in QUEUES:
            if self.dmas[q]:
                self._wait("gp", [("d", q, 16 * self.dmas[q], {})])

    def emit(self, nc, sems):
        streams = {"te": "tensor", "ve": "vector", "se": "scalar",
                   "gp": "gpsimd", "sy": "sync"}
        with nc.Block() as block:
            for s, engname in streams.items():
                items = self.items[s]
                if not items:
                    continue
                dec = getattr(block, engname)

                def body(eng, items=items):
                    for it in items:
                        if it[0] == "wait":
                            eng.wait_ge(sems[it[1]], it[2])
                        else:
                            it[1](eng).then_inc(sems[it[2]], it[3])
                dec(body)


DBG = False


def build_program(nc):
    p = Prog()
    st = ExitStack()

    def din(name, shape, dt=F32):
        return nc.dram_tensor(name, list(shape), dt, kind="ExternalInput").ap()

    x = din("x", (T, D))
    Wqkv = din("Wqkv", (D, 3 * D))
    Wq = din("Wq", (D, D)); Wk = din("Wk", (D, D)); Wv = din("Wv", (D, D))
    Wo = din("Wo", (D, D)); Wproj = din("Wproj", (D, D))
    Wr = din("Wr", (D, E))
    ew1 = din("ew1", (E, D, FF)); ew2 = din("ew2", (E, FF, D))
    eb2 = din("eb2", (E, D))
    bqkv_c = din("bqkv_c", (128, 24))
    bq_c = din("bq_c", (128, 8)); bk_c = din("bk_c", (128, 8))
    l1g_c = din("l1g_c", (128, 8)); l1b_c = din("l1b_c", (128, 8))
    l2g_c = din("l2g_c", (128, 8)); l2b_c = din("l2b_c", (128, 8))
    b1cols = din("b1cols", (128, 256))
    bqkv_vrow = din("bqkv_vrow", (1, D)); bv_row = din("bv_row", (1, D))
    bo_row = din("bo_row", (1, D)); bproj_row = din("bproj_row", (1, D))
    br_row = din("br_row", (1, E))
    ident = din("ident", (128, 128))
    ones_colf = din("ones_colf", (128, 1))
    ones_rowf = din("ones_rowf", (1, 128))
    ones_bf = din("ones_bf", (1, 2048), BF16)
    ones128_bf = din("ones128_bf", (1, 128), BF16)
    masks_in = din("masks", (8, 128, 512), BF16)
    rsig_bc = din("rsig_bc", (128, 1))

    out = nc.dram_tensor("out", [NOWN, D], F32, kind="ExternalOutput").ap()

    def scr(name, shape, dt=BF16):
        return nc.dram_tensor(name, list(shape), dt, kind="Internal").ap()

    s_xnT = scr("s_xnT", (8, 128, T))
    s_hT = scr("s_hT", (8, 128, T))
    s_qr = scr("s_qr", (H, 66, T))
    s_kh = scr("s_kh", (H, 66, NOWN))
    s_k = scr("s_k", (H, 64, T))
    s_q = scr("s_q", (H, 64, NOWN))
    s_v = scr("s_v", (2, 32, 128, 512))
    s_vr = scr("s_vr", (2, 32, 128, 512))
    s_a = scr("s_a", (NB, 128, D), F32)

    NBF, NFA, NFB = 66560, 7168, 8192
    abf = st.enter_context(nc.sbuf_tensor("abf", [128, NBF], BF16))
    afa = st.enter_context(nc.sbuf_tensor("afa", [128, NFA], F32))
    afb = st.enter_context(nc.sbuf_tensor("afb", [128, NFB], F32))

    o_big0 = 0
    o_wsb = 32768
    o_own = 40960
    o_bfst = [45056, 45568]
    o_oT = 0
    o_kb = [16384, 20480]
    o_vb = [24576, 26656]
    o_qs = [28736, 29248]
    o_pbuf = [29760, 30784]
    o_tstage = [40960, 41984]
    o_masks = 40960
    o_mid = 0
    o_h2T = 32768
    o_w2b = 40960
    o_w1b = [57344, 58368]
    o_wT = 59392
    o_kh_b = [0, 2048]
    o_qh_b = [4096, 8192]
    o_vrb = [12288, 14336]
    o_orT = 16384
    o_eb = [41984, 42496]
    sm = 61440

    def bfslot(n):
        nonlocal sm
        o = sm; sm += n
        assert sm <= NBF
        return o

    o_ones128 = bfslot(128)
    o_rows = bfslot(1024)
    o_rows2 = bfslot(1024)
    o_brrow = bfslot(8)
    o_eb2 = bfslot(1024)
    o_Wr = bfslot(64)

    o_xb = [0, 1024]
    o_sq = 2048
    o_xc = 3072
    o_xn1 = 4096
    o_h1 = 5120
    fa = 6144

    def faslot(n):
        nonlocal fa
        o = fa; fa += n
        assert fa <= NFA
        return o

    o_bqkvc = faslot(24)
    o_bqc = faslot(8); o_bkc = faslot(8)
    o_l1g = faslot(8); o_l1b = faslot(8)
    o_l2g = faslot(8); o_l2b = faslot(8)
    o_b1c = faslot(256)
    o_ident = faslot(128)
    o_onescf = faslot(1)
    o_onesrf = faslot(128)
    o_rsig = faslot(1)
    o_rsig2 = faslot(1)
    o_nrsig = faslot(1)
    o_red = faslot(8)
    o_wrt = faslot(64)

    _pb_single = {i: st.enter_context(nc.psum_tensor(f"pb{i}", [128, 512],
                                                     F32))
                  for i in (0, 1, 4, 5, 6, 7)}
    pb23 = st.enter_context(nc.psum_tensor("pb23", [128, 1024], F32))
    pb = [_pb_single[0], _pb_single[1], pb23[0:128, 0:512],
          pb23[0:128, 512:1024], _pb_single[4], _pb_single[5],
          _pb_single[6], _pb_single[7]]
    PK = ["p0", "p1", "p2", "p3", "pv0", "pv1", "p6", "p7"]

    sems = {}
    for e in ENGS:
        sems[("t", e)] = st.enter_context(nc.semaphore(f"tick_{e}"))
    for q in QUEUES:
        sems[("d", q)] = st.enter_context(nc.semaphore(f"dma_{q}"))

    def B_(off, cols, p0=0, np_=128):
        return abf[p0:p0 + np_, off:off + cols]

    def FA(off, cols, p0=0, np_=128):
        return afa[p0:p0 + np_, off:off + cols]

    def FB(off, cols, p0=0, np_=128):
        return afb[p0:p0 + np_, off:off + cols]

    def ld(dst, src, key):
        p.dma("sy", lambda e, d=dst, s=src: e.dma_start(out=d, in_=s),
              reads=(), writes=(key,))

    ld(B_(o_ones128, 128, 0, 1), ones128_bf[0:1, :], "c_ones128")
    ld(B_(o_ones128, 128, 32, 1), ones128_bf[0:1, :], "c_ones128")
    ld(B_(o_ones128, 128, 64, 1), ones128_bf[0:1, :], "c_ones128")
    ld(FA(o_bqkvc, 24), bqkv_c[:, :], "c_bqkvc")
    ld(FA(o_bqc, 8), bq_c[:, :], "c_bqc")
    ld(FA(o_bkc, 8), bk_c[:, :], "c_bkc")
    ld(FA(o_l1g, 8), l1g_c[:, :], "c_l1g")
    ld(FA(o_l1b, 8), l1b_c[:, :], "c_l1b")
    ld(FA(o_l2g, 8), l2g_c[:, :], "c_l2g")
    ld(FA(o_l2b, 8), l2b_c[:, :], "c_l2b")
    ld(FA(o_b1c, 256), b1cols[:, :], "c_b1c")
    ld(FA(o_ident, 128), ident[:, :], "c_ident")
    ld(FA(o_onescf, 1), ones_colf[:, :], "c_onescf")
    ld(FA(o_onesrf, 128, 0, 1), ones_rowf[0:1, :], "c_onesrf")
    ld(FA(o_onesrf, 128, 64, 1), ones_rowf[0:1, :], "c_onesrf")
    ld(FA(o_rsig, 1), rsig_bc[:, :], "c_rsig")

    def row_bf(src, off, p0, key):
        p.dma("sy", lambda e, s=src, p0=p0: e.dma_start(
            out=FA(o_xb[0], 1024, p0, 1), in_=s),
            reads=(), writes=("xb0",))
        p.op("ve", lambda e, off=off, p0=p0: e.tensor_copy(
            out=B_(off, 1024, p0, 1), in_=FA(o_xb[0], 1024, p0, 1)),
            reads=("xb0",), writes=(key,))

    row_bf(bqkv_vrow[0:1, :], o_rows, 0, "c_bqv")
    row_bf(bv_row[0:1, :], o_rows, 32, "c_bv")
    row_bf(bo_row[0:1, :], o_rows, 64, "c_bo")
    row_bf(bproj_row[0:1, :], o_rows2, 0, "c_bp")
    p.dma("sy", lambda e: e.dma_start(out=FA(o_xb[1], 8, 0, 1),
                                      in_=br_row[0:1, :]),
          reads=(), writes=("xb1",))
    p.op("ve", lambda e: e.tensor_copy(out=B_(o_brrow, 8, 0, 1),
                                       in_=FA(o_xb[1], 8, 0, 1)),
         reads=("xb1",), writes=("c_br",))
    p.dma("sy", lambda e: e.dma_start(out=FB(0, 1024, 0, 8), in_=eb2[:, :]),
          reads=(), writes=("eb2f",))
    p.op("ve", lambda e: e.tensor_copy(out=B_(o_eb2, 1024, 0, 8),
                                       in_=FB(0, 1024, 0, 8)),
         reads=("eb2f",), writes=("c_eb2",))
    p.dma("sy", lambda e: e.dma_start(
        out=FB(1024, 64).rearrange("q (dt e) -> q dt e", e=8),
        in_=Wr[:, :].rearrange("(dt q) e -> q dt e", q=128)),
        reads=(), writes=("wrf",))
    p.op("ve", lambda e: e.tensor_copy(out=B_(o_Wr, 64), in_=FB(1024, 64)),
         reads=("wrf",), writes=("c_Wr",))
    p.op("ve", lambda e: e.tensor_scalar_mul(out=FA(o_rsig2, 1),
                                             in0=FA(o_rsig, 1), scalar1=2.0),
         reads=("c_rsig",), writes=("c_rsig2",))
    p.op("ve", lambda e: e.tensor_scalar_mul(out=FA(o_nrsig, 1),
                                             in0=FA(o_rsig, 1), scalar1=-1.0),
         reads=("c_rsig",), writes=("c_nrsig",))

    WK = tuple(f"W{d}" for d in range(8))
    OWNK = tuple(f"own{d}" for d in range(8))
    XNT = tuple(f"xnT{d}" for d in range(8))
    HT = tuple(f"hT{d}" for d in range(8))
    TSK = tuple(f"ts{i}_{d}" for i in range(2) for d in range(8))

    def ln_chain(xkey, xoff, outoff, outkey):
        r = o_red
        p.op("ve", lambda e: e.reduce_sum(out=FA(r, 1), in_=FA(xoff, 1024),
                                          axis=AX.X, negate=True),
             reads=(xkey,), writes=("red0",))
        p.op("ve", lambda e: e.tensor_scalar_mul(out=FA(r, 1), in0=FA(r, 1),
                                                 scalar1=1.0 / 1024),
             reads=("red0",), writes=("red0",))
        p.op("ve", lambda e: e.tensor_scalar_add(out=FA(o_xc, 1024),
                                                 in0=FA(xoff, 1024),
                                                 scalar1=FA(r, 1)),
             reads=(xkey, "red0"), writes=("xc",))
        p.op("se", lambda e: e.activation(out=FA(o_sq, 1024),
                                          in_=FA(o_xc, 1024),
                                          func=AF.Square),
             reads=("xc",), writes=("sq",))
        p.op("ve", lambda e: e.reduce_sum(out=FA(r + 1, 1),
                                          in_=FA(o_sq, 1024), axis=AX.X),
             reads=("sq",), writes=("red1",))
        p.op("ve", lambda e: e.tensor_scalar(
            out=FA(r + 4, 1), in0=FA(r + 1, 1), scalar1=1.0 / 1024,
            scalar2=EPS, op0=ALU.mult, op1=ALU.add),
            reads=("red1",), writes=("red4",))
        p.op("se", lambda e: e.activation(out=FA(r + 5, 1), in_=FA(r + 4, 1),
                                          func=AF.Sqrt),
             reads=("red4",), writes=("red5",))
        p.op("ve", lambda e: e.reciprocal(out=FA(r + 2, 1), in_=FA(r + 5, 1)),
             reads=("red5",), writes=("red2",))
        p.op("ve", lambda e: e.tensor_scalar_mul(out=FA(outoff, 1024),
                                                 in0=FA(o_xc, 1024),
                                                 scalar1=FA(r + 2, 1)),
             reads=("xc", "red2"), writes=(outkey,))

    for it in range(32):
        xk = f"xb{it % 2}"
        p.dma("sy", lambda e, it=it: e.dma_start(
            out=FA(o_xb[it % 2], 1024), in_=x[it * 128:(it + 1) * 128, :]),
            reads=(), writes=(xk,))
        ln_chain(xk, o_xb[it % 2], o_xn1, "xn1")
        ln_chain("xn1", o_xn1, o_h1, "h1")
        for dt in range(8):
            p.op("te", lambda e, dt=dt: e.transpose(
                out=pb[6][0:128, 0:128],
                in_=afa[:, o_xn1 + dt * 128:o_xn1 + dt * 128 + 128],
                identity=FA(o_ident, 128)),
                reads=("xn1", "c_ident"), writes=("p6",))
            p.op("se", lambda e, dt=dt: e.copy(
                out=B_(o_tstage[0] + dt * 128, 128), in_=pb[6][0:128, 0:128]),
                reads=("p6",), writes=(f"ts0_{dt}",))
        for dt in range(8):
            p.dma("sy", lambda e, dt=dt, it=it: e.dma_start(
                out=s_xnT[dt, :, it * 128:(it + 1) * 128],
                in_=B_(o_tstage[0] + dt * 128, 128)),
                reads=(f"ts0_{dt}",), writes=())
        for dt in range(8):
            p.op("te", lambda e, dt=dt: e.transpose(
                out=pb[6][0:128, 0:128],
                in_=afa[:, o_h1 + dt * 128:o_h1 + dt * 128 + 128],
                identity=FA(o_ident, 128)),
                reads=("h1", "c_ident"), writes=("p6",))
            p.op("ve", lambda e, dt=dt: e.tensor_scalar(
                out=B_(o_tstage[1] + dt * 128, 128),
                in0=pb[6][0:128, 0:128],
                scalar1=afa[:, o_l1g + dt:o_l1g + dt + 1],
                scalar2=afa[:, o_l1b + dt:o_l1b + dt + 1],
                op0=ALU.mult, op1=ALU.add),
                reads=("p6", "c_l1g", "c_l1b"), writes=(f"ts1_{dt}",))
        for dt in range(8):
            p.dma("sy", lambda e, dt=dt, it=it: e.dma_start(
                out=s_hT[dt, :, it * 128:(it + 1) * 128],
                in_=B_(o_tstage[1] + dt * 128, 128)),
                reads=(f"ts1_{dt}",), writes=())

    def load_w_pass(Wsrc, col0, rdeps=()):
        for dt in range(8):
            wb = dt % 2
            p.dma("vq", lambda e, dt=dt, wb=wb: e.dma_start(
                out=FA(o_xb[wb], 1024),
                in_=Wsrc[dt * 128:(dt + 1) * 128, col0:col0 + 1024]),
                reads=(), writes=(f"xb{wb}",))
            p.op("ve", lambda e, dt=dt, wb=wb: e.tensor_copy(
                out=B_(o_wsb + dt * 1024, 1024), in_=FA(o_xb[wb], 1024)),
                reads=(f"xb{wb}",) + tuple(rdeps), writes=(f"W{dt}",))

    def ownc(dt, oc):
        ap5 = abf[:, o_big0:o_big0 + 32768].rearrange(
            "q (d m par z) -> q d par m z", d=8, m=16, par=2, z=128)
        return ap5[:, dt, 0, 4 * oc:4 * oc + 4, :]

    for dt in range(8):
        p.dma("sy", lambda e, dt=dt: e.dma_start(
            out=B_(o_big0 + dt * 4096, 4096), in_=s_xnT[dt, :, :]),
            reads=(), writes=(f"xnT{dt}",))

    load_w_pass(Wqkv, 0)
    for nt in range(8):
        for ic in range(8):
            bank, bk = pb[ic % 2], PK[ic % 2]

            def mmg(e, nt=nt, ic=ic, bank=bank):
                for dt in range(8):
                    inst = e.matmul(
                        out=bank[0:128, 0:512],
                        lhsT=B_(o_wsb + dt * 1024 + nt * 128, 128),
                        rhs=B_(o_big0 + dt * 4096 + ic * 512, 512),
                        start=(dt == 0), stop=(dt == 7))
                return inst
            p.op("te", mmg, reads=XNT + WK, writes=(bk,))
            p.op("ve", lambda e, nt=nt, bank=bank: e.tensor_scalar_add(
                out=FA(o_xc, 512), in0=bank[0:128, 0:512],
                scalar1=afa[:, o_bqkvc + nt:o_bqkvc + nt + 1]),
                reads=(bk, "c_bqkvc"), writes=("xc",))
            p.op("ve", lambda e: e.tensor_scalar_mul(
                out=B_(o_bfst[0], 512), in0=FA(o_xc, 512),
                scalar1=FA(o_rsig2, 1)),
                reads=("xc", "c_rsig2"), writes=("bfst0",))
            p.op("se", lambda e: e.activation(out=FA(o_sq, 512),
                                              in_=FA(o_xc, 512),
                                              func=AF.Square),
                 reads=("xc",), writes=("sq",))
            for hh in range(2):
                h = 2 * nt + hh
                p.op("te", lambda e, hh=hh: e.matmul(
                    out=pb[7][0:1, 0:512],
                    lhsT=afa[hh * 64:hh * 64 + 64, o_onescf:o_onescf + 1],
                    rhs=afa[hh * 64:hh * 64 + 64, o_sq:o_sq + 512],
                    start=True, stop=True),
                    reads=("sq", "c_onescf"), writes=("p7",))
                p.op("ve", lambda e: e.tensor_scalar_mul(
                    out=B_(o_bfst[1], 512, 0, 1), in0=pb[7][0:1, 0:512],
                    scalar1=afa[0:1, o_nrsig:o_nrsig + 1]),
                    reads=("p7", "c_nrsig"), writes=("bfst1",))
                p.dma("sy", lambda e, h=h, ic=ic: e.dma_start(
                    out=s_qr[h, 65:66, ic * 512:(ic + 1) * 512],
                    in_=B_(o_bfst[1], 512, 0, 1)),
                    reads=("bfst1",), writes=())
                p.dma("sy", lambda e, h=h, hh=hh, ic=ic: e.dma_start(
                    out=s_qr[h, 0:64, ic * 512:(ic + 1) * 512],
                    in_=B_(o_bfst[0], 512, hh * 64, 64)),
                    reads=("bfst0",), writes=())
    for h in range(H):
        p.dma("sy", lambda e, h=h: e.dma_start(
            out=s_qr[h, 64:65, 0:2048], in_=ones_bf[0:1, :]),
            reads=(), writes=())
        p.dma("sy", lambda e, h=h: e.dma_start(
            out=s_qr[h, 64:65, 2048:4096], in_=ones_bf[0:1, :]),
            reads=(), writes=())

    load_w_pass(Wqkv, 1024)
    for oc in range(4):
        for dt in range(8):
            p.op("ve", lambda e, dt=dt, oc=oc: e.tensor_copy(
                out=B_(o_own + dt * 512, 512).rearrange(
                    "q (j z) -> q j z", z=128),
                in_=ownc(dt, oc)),
                reads=XNT + (TSK if oc == 0 else ()),
                writes=(f"own{dt}",))
        for nt in range(8):
            bank, bk = pb[nt % 2], PK[nt % 2]

            def mmk(e, nt=nt, bank=bank):
                for dt in range(8):
                    inst = e.matmul(
                        out=bank[0:128, 0:512],
                        lhsT=B_(o_wsb + dt * 1024 + nt * 128, 128),
                        rhs=B_(o_own + dt * 512, 512),
                        start=(dt == 0), stop=(dt == 7))
                return inst
            p.op("te", mmk, reads=OWNK + WK, writes=(bk,))
            p.op("ve", lambda e, nt=nt, bank=bank: e.tensor_scalar_add(
                out=FA(o_xc, 512), in0=bank[0:128, 0:512],
                scalar1=afa[:, o_bqkvc + 8 + nt:o_bqkvc + 9 + nt]),
                reads=(bk, "c_bqkvc"), writes=("xc",))
            p.op("ve", lambda e: e.tensor_copy(
                out=B_(o_bfst[0], 512), in_=FA(o_xc, 512)),
                reads=("xc",), writes=("bfst0",))
            p.op("se", lambda e: e.activation(out=FA(o_sq, 512),
                                              in_=FA(o_xc, 512),
                                              func=AF.Square),
                 reads=("xc",), writes=("sq",))
            for hh in range(2):
                h = 2 * nt + hh
                p.op("te", lambda e, hh=hh: e.matmul(
                    out=pb[7][0:1, 0:512],
                    lhsT=afa[hh * 64:hh * 64 + 64, o_onescf:o_onescf + 1],
                    rhs=afa[hh * 64:hh * 64 + 64, o_sq:o_sq + 512],
                    start=True, stop=True),
                    reads=("sq", "c_onescf"), writes=("p7",))
                p.op("ve", lambda e: e.tensor_scalar_mul(
                    out=B_(o_bfst[1], 512, 0, 1), in0=pb[7][0:1, 0:512],
                    scalar1=afa[0:1, o_nrsig:o_nrsig + 1]),
                    reads=("p7", "c_nrsig"), writes=("bfst1",))
                p.dma("sy", lambda e, h=h, oc=oc: e.dma_start(
                    out=s_kh[h, 64:65, oc * 512:(oc + 1) * 512],
                    in_=B_(o_bfst[1], 512, 0, 1)),
                    reads=("bfst1",), writes=())
                p.dma("sy", lambda e, h=h, hh=hh, oc=oc: e.dma_start(
                    out=s_kh[h, 0:64, oc * 512:(oc + 1) * 512],
                    in_=B_(o_bfst[0], 512, hh * 64, 64)),
                    reads=("bfst0",), writes=())
    for h in range(H):
        p.dma("sy", lambda e, h=h: e.dma_start(
            out=s_kh[h, 65:66, 0:2048], in_=ones_bf[0:1, :]),
            reads=(), writes=())

    load_w_pass(Wqkv, 2048)
    for it in range(32):
        for nc2 in range(2):
            bank, bk = pb[(it * 2 + nc2) % 2], PK[(it * 2 + nc2) % 2]

            def mmv(e, it=it, nc2=nc2, bank=bank):
                e.matmul(out=bank[0:128, 0:512],
                         lhsT=B_(o_ones128, 128, 0, 1),
                         rhs=B_(o_rows + nc2 * 512, 512, 0, 1),
                         start=True, stop=False)
                for dt in range(8):
                    inst = e.matmul(
                        out=bank[0:128, 0:512],
                        lhsT=B_(o_big0 + dt * 4096 + it * 128, 128),
                        rhs=B_(o_wsb + dt * 1024 + nc2 * 512, 512),
                        start=False, stop=(dt == 7))
                return inst
            p.op("te", mmv, reads=XNT + WK + ("c_ones128", "c_bqv"),
                 writes=(bk,))
            p.op("se", lambda e, bank=bank: e.copy(
                out=B_(o_bfst[0], 512), in_=bank[0:128, 0:512]),
                reads=(bk,), writes=("bfst0",))
            p.dma("sy", lambda e, it=it, nc2=nc2: e.dma_start(
                out=s_vr[nc2, it, :, :], in_=B_(o_bfst[0], 512)),
                reads=("bfst0",), writes=())

    for dt in range(8):
        p.dma("sy", lambda e, dt=dt: e.dma_start(
            out=B_(o_big0 + dt * 4096, 4096), in_=s_hT[dt, :, :]),
            reads=(), writes=(f"hT{dt}", f"xnT{dt}"))

    load_w_pass(Wk, 0)
    for nt in range(8):
        for ic in range(8):
            bank, bk = pb[ic % 2], PK[ic % 2]

            def mmck(e, nt=nt, ic=ic, bank=bank):
                for dt in range(8):
                    inst = e.matmul(
                        out=bank[0:128, 0:512],
                        lhsT=B_(o_wsb + dt * 1024 + nt * 128, 128),
                        rhs=B_(o_big0 + dt * 4096 + ic * 512, 512),
                        start=(dt == 0), stop=(dt == 7))
                return inst
            p.op("te", mmck, reads=HT + WK, writes=(bk,))
            p.op("ve", lambda e, nt=nt, bank=bank: e.tensor_scalar(
                out=B_(o_bfst[0], 512), in0=bank[0:128, 0:512],
                scalar1=afa[:, o_bkc + nt:o_bkc + nt + 1], scalar2=None,
                op0=ALU.add),
                reads=(bk, "c_bkc"), writes=("bfst0",))
            for hh in range(2):
                p.dma("sy", lambda e, nt=nt, hh=hh, ic=ic: e.dma_start(
                    out=s_k[2 * nt + hh, :, ic * 512:(ic + 1) * 512],
                    in_=B_(o_bfst[0], 512, hh * 64, 64)),
                    reads=("bfst0",), writes=())

    load_w_pass(Wq, 0)
    for oc in range(4):
        for dt in range(8):
            p.op("ve", lambda e, dt=dt, oc=oc: e.tensor_copy(
                out=B_(o_own + dt * 512, 512).rearrange(
                    "q (j z) -> q j z", z=128),
                in_=ownc(dt, oc)),
                reads=HT, writes=(f"own{dt}",))
        for nt in range(8):
            bank, bk = pb[nt % 2], PK[nt % 2]

            def mmcq(e, nt=nt, bank=bank):
                for dt in range(8):
                    inst = e.matmul(
                        out=bank[0:128, 0:512],
                        lhsT=B_(o_wsb + dt * 1024 + nt * 128, 128),
                        rhs=B_(o_own + dt * 512, 512),
                        start=(dt == 0), stop=(dt == 7))
                return inst
            p.op("te", mmcq, reads=OWNK + WK, writes=(bk,))
            p.op("ve", lambda e, nt=nt, bank=bank: e.tensor_scalar(
                out=B_(o_bfst[0], 512), in0=bank[0:128, 0:512],
                scalar1=afa[:, o_bqc + nt:o_bqc + nt + 1], scalar2=0.125,
                op0=ALU.add, op1=ALU.mult),
                reads=(bk, "c_bqc"), writes=("bfst0",))
            for hh in range(2):
                p.dma("sy", lambda e, nt=nt, hh=hh, oc=oc: e.dma_start(
                    out=s_q[2 * nt + hh, :, oc * 512:(oc + 1) * 512],
                    in_=B_(o_bfst[0], 512, hh * 64, 64)),
                    reads=("bfst0",), writes=())

    load_w_pass(Wv, 0)
    for it in range(32):
        for nc2 in range(2):
            bank, bk = pb[(it * 2 + nc2) % 2], PK[(it * 2 + nc2) % 2]

            def mmcv(e, it=it, nc2=nc2, bank=bank):
                e.matmul(out=bank[0:128, 0:512],
                         lhsT=B_(o_ones128, 128, 32, 1),
                         rhs=B_(o_rows + nc2 * 512, 512, 32, 1),
                         start=True, stop=False)
                for dt in range(8):
                    inst = e.matmul(
                        out=bank[0:128, 0:512],
                        lhsT=B_(o_big0 + dt * 4096 + it * 128, 128),
                        rhs=B_(o_wsb + dt * 1024 + nc2 * 512, 512),
                        start=False, stop=(dt == 7))
                return inst
            p.op("te", mmcv, reads=HT + WK + ("c_ones128", "c_bv"),
                 writes=(bk,))
            p.op("se", lambda e, bank=bank: e.copy(
                out=B_(o_bfst[0], 512), in_=bank[0:128, 0:512]),
                reads=(bk,), writes=("bfst0",))
            p.dma("sy", lambda e, it=it, nc2=nc2: e.dma_start(
                out=s_v[nc2, it, :, :], in_=B_(o_bfst[0], 512)),
                reads=("bfst0",), writes=())

    if DBG:
        dbg_outs = {
            "d_xn": nc.dram_tensor("d_xn", [128, T], BF16,
                                   kind="ExternalOutput").ap(),
            "d_h": nc.dram_tensor("d_h", [128, T], BF16,
                                  kind="ExternalOutput").ap(),
            "d_qr": nc.dram_tensor("d_qr", [66, T], BF16,
                                   kind="ExternalOutput").ap(),
            "d_kh": nc.dram_tensor("d_kh", [66, NOWN], BF16,
                                   kind="ExternalOutput").ap(),
            "d_k": nc.dram_tensor("d_k", [64, T], BF16,
                                  kind="ExternalOutput").ap(),
            "d_q": nc.dram_tensor("d_q", [64, NOWN], BF16,
                                  kind="ExternalOutput").ap(),
            "d_v": nc.dram_tensor("d_v", [32, 128, 512], BF16,
                                  kind="ExternalOutput").ap(),
            "d_oT": nc.dram_tensor("d_oT", [128, 16384], BF16,
                                   kind="ExternalOutput").ap(),
            "d_a": nc.dram_tensor("d_a", [NB, 128, D], F32,
                                  kind="ExternalOutput").ap(),
            "d_w": nc.dram_tensor("d_w", [128, 64], F32,
                                  kind="ExternalOutput").ap(),
            "d_part": nc.dram_tensor("d_part", [NB, 128, D], F32,
                                     kind="ExternalOutput").ap(),
            "d_orT": nc.dram_tensor("d_orT", [128, 16384], BF16,
                                    kind="ExternalOutput").ap(),
        }
        for nm, src in [("d_xn", s_xnT[0, :, :]), ("d_h", s_hT[0, :, :]),
                        ("d_qr", s_qr[0, :, :]), ("d_kh", s_kh[0, :, :]),
                        ("d_k", s_k[0, :, :]), ("d_q", s_q[0, :, :]),
                        ("d_v", s_v[0][:, :, 0:512])]:
            p.dma("sy", lambda e, d=dbg_outs[nm], s=src: e.dma_start(
                out=d[:, :] if len(d.shape) == 2 else d[:, :, :], in_=s),
                reads=(), writes=())

    p.op("ve", lambda e: e.memset(FA(o_red + 7, 1), 0.0),
         reads=(), writes=HT + XNT + OWNK + TSK + ("bfst0", "bfst1", "PH2"))

    p.dma("sy", lambda e: e.dma_start(
        out=B_(o_masks, 4096).rearrange("q (m n) -> q m n", n=512),
        in_=masks_in[:, :, :].rearrange("m q n -> q m n")),
        reads=("PH2",), writes=("c_masks",))
    for hb in range(2):
        p.op("ve", lambda e, hb=hb: e.memset(
            abf[:, o_vb[hb]:o_vb[hb] + 2080].rearrange(
                "q (j n) -> q j n", n=65)[:, :, 64:65], 1.0),
            reads=("PH2",), writes=(f"vones{hb}",))

    for h in range(H):
        hb = h % 2
        p.dma("sy", lambda e, h=h, hb=hb: e.dma_start(
            out=B_(o_kb[hb], 4096, 0, 64), in_=s_k[h, :, :]),
            reads=("PH2",), writes=(f"kb{hb}",))
        p.dma("sy", lambda e, h=h, hb=hb: e.dma_start(
            out=abf[:, o_vb[hb]:o_vb[hb] + 2080].rearrange(
                "q (j n) -> q j n", n=65)[:, :, 0:64],
            in_=s_v[h // 8][:, :, (h % 8) * 64:(h % 8) * 64 + 64].rearrange(
                "j q n -> q j n")),
            reads=(f"vones{hb}",), writes=(f"vb{hb}",))
        for oc in range(4):
            qb = (h * 4 + oc) % 2
            p.dma("sy", lambda e, h=h, oc=oc, qb=qb: e.dma_start(
                out=B_(o_qs[qb], 512, 0, 64),
                in_=s_q[h, :, oc * 512:(oc + 1) * 512]),
                reads=("PH2",), writes=(f"qs{qb}",))
            pvbank, pvk = pb[4 + (h * 4 + oc) % 2], PK[4 + (h * 4 + oc) % 2]
            nj = 8 * oc + 8
            for Jp in range(nj // 2):
                J0 = 2 * Jp

                def mmS(e, J0=J0, hb=hb, qb=qb):
                    for dj in range(2):
                        inst = e.matmul(
                            out=pb23[0:128, dj * 512:dj * 512 + 512],
                            lhsT=B_(o_kb[hb] + (J0 + dj) * 128, 128, 0, 64),
                            rhs=B_(o_qs[qb], 512, 0, 64),
                            start=True, stop=True)
                    return inst
                p.op("te", mmS, reads=(f"kb{hb}", f"qs{qb}"),
                     writes=("p2", "p3"))
                if J0 + 2 <= 8 * oc:
                    p.op("se", lambda e: e.activation(
                        out=B_(o_pbuf[0], 1024), in_=pb23[0:128, 0:1024],
                        func=AF.Exp),
                        reads=("p2", "p3", "PH2"), writes=("pbf",))
                else:
                    r = J0 - 8 * oc
                    p.op("se", lambda e: e.activation(
                        out=FA(o_xc, 1024), in_=pb23[0:128, 0:1024],
                        func=AF.Exp),
                        reads=("p2", "p3"), writes=("xc",))
                    p.op("ve", lambda e, r=r: e.tensor_tensor(
                        out=B_(o_pbuf[0], 1024), in0=FA(o_xc, 1024),
                        in1=B_(o_masks + r * 512, 1024), op=ALU.mult),
                        reads=("xc", "c_masks", "PH2"), writes=("pbf",))

                def mmPV(e, J0=J0, hb=hb, nj=nj, pvbank=pvbank):
                    for dj in range(2):
                        J = J0 + dj
                        inst = e.matmul(
                            out=pvbank[0:65, 0:512],
                            lhsT=abf[:, o_vb[hb] + J * 65:
                                     o_vb[hb] + J * 65 + 65],
                            rhs=B_(o_pbuf[0] + dj * 512, 512),
                            start=(J == 0), stop=(J == nj - 1))
                    return inst
                p.op("te", mmPV,
                     reads=(f"vb{hb}", f"vones{hb}", "pbf"), writes=(pvk,))
            p.op("ve", lambda e, pvbank=pvbank: e.reciprocal(
                out=FA(o_xc, 512, 64, 1), in_=pvbank[64:65, 0:512]),
                reads=(pvk,), writes=("xc",))
            p.op("te", lambda e: e.matmul(
                out=pb[6][0:64, 0:512],
                lhsT=afa[64:65, o_onesrf:o_onesrf + 64],
                rhs=FA(o_xc, 512, 64, 1), start=True, stop=True),
                reads=("xc", "c_onesrf"), writes=("p6",))
            p.op("se", lambda e: e.copy(out=FA(o_xb[0], 512, 0, 64),
                                        in_=pb[6][0:64, 0:512]),
                 reads=("p6",), writes=("xb0",))
            p.op("ve", lambda e, pvbank=pvbank: e.tensor_tensor(
                out=B_(o_bfst[0], 512, 0, 64),
                in0=pvbank[0:64, 0:512], in1=FA(o_xb[0], 512, 0, 64),
                op=ALU.mult),
                reads=(pvk, "xb0", "PH2"), writes=("bfst0",))
            p.dma("sy", lambda e, h=h, oc=oc: e.dma_start(
                out=abf[64 * (h % 2):64 * (h % 2) + 64,
                        o_oT + (h // 2) * 2048 + oc * 512:
                        o_oT + (h // 2) * 2048 + oc * 512 + 512],
                in_=B_(o_bfst[0], 512, 0, 64)),
                reads=("bfst0", "PH2"), writes=(f"oT{h}_{oc}",))
    OT = tuple(f"oT{h}_{oc}" for h in range(H) for oc in range(4))

    if DBG:
        p.dma("sy", lambda e: e.dma_start(
            out=dbg_outs["d_oT"][:, :], in_=abf[:, o_oT:o_oT + 16384]),
            reads=OT, writes=())

    load_w_pass(Wo, 0)
    for it in range(NB):
        r0 = 256 * it
        xk = f"xb{it % 2}"
        p.dma("sy", lambda e, r0=r0, it=it: e.dma_start(
            out=FA(o_xb[it % 2], 1024), in_=x[r0:r0 + 128, :]),
            reads=(), writes=(xk,))
        ln_chain(xk, o_xb[it % 2], o_xn1, "xn1")
        for nc2 in range(2):
            bank, bk = pb[nc2], PK[nc2]

            def mma(e, it=it, nc2=nc2, bank=bank):
                e.matmul(out=bank[0:128, 0:512],
                         lhsT=B_(o_ones128, 128, 64, 1),
                         rhs=B_(o_rows + nc2 * 512, 512, 64, 1),
                         start=True, stop=False)
                for dt in range(8):
                    inst = e.matmul(
                        out=bank[0:128, 0:512],
                        lhsT=B_(o_oT + dt * 2048 + it * 128, 128),
                        rhs=B_(o_wsb + dt * 1024 + nc2 * 512, 512),
                        start=False, stop=(dt == 7))
                return inst
            p.op("te", mma, reads=OT + WK + ("c_ones128", "c_bo"),
                 writes=(bk,))
            p.op("ve", lambda e, nc2=nc2, bank=bank: e.tensor_tensor(
                out=afa[:, o_xc + nc2 * 512:o_xc + nc2 * 512 + 512],
                in0=bank[0:128, 0:512],
                in1=afa[:, o_xn1 + nc2 * 512:o_xn1 + nc2 * 512 + 512],
                op=ALU.add),
                reads=(bk, "xn1"), writes=("xc",))
        p.dma("sy", lambda e, it=it: e.dma_start(
            out=s_a[it, :, :], in_=FA(o_xc, 1024)),
            reads=("xc",), writes=(f"sa{it}",))
        if DBG:
            p.dma("sy", lambda e, it=it: e.dma_start(
                out=dbg_outs["d_a"][it, :, :], in_=FA(o_xc, 1024)),
                reads=("xc",), writes=())

    MIDK = tuple(f"mid{f}_{c}" for f in range(32) for c in range(2))
    H2K = tuple(f"h2T{d}_{i}" for d in range(8) for i in range(8))
    W2K = tuple(f"w2b{f}" for f in range(32))

    p.op("ve", lambda e: e.memset(FA(o_red + 7, 1), 0.0),
         reads=(), writes=OT + WK +
         ("kb0", "kb1", "vb0", "vb1", "vones0", "vones1", "qs0", "qs1",
          "pbf0", "pbf1", "c_masks", "eb2f", "wrf", "PH3"))

    for half in range(2):
        for itl in range(8):
            it = half * 8 + itl
            xk = f"xb{itl % 2}"
            p.dma("sy", lambda e, it=it, itl=itl: e.dma_start(
                out=FA(o_xb[itl % 2], 1024), in_=s_a[it, :, :]),
                reads=(f"sa{it}",), writes=(xk,))
            ln_chain(xk, o_xb[itl % 2], o_xn1, "xn1")
            for dt in range(8):
                p.op("te", lambda e, dt=dt: e.transpose(
                    out=pb[6][0:128, 0:128],
                    in_=afa[:, o_xn1 + dt * 128:o_xn1 + dt * 128 + 128],
                    identity=FA(o_ident, 128)),
                    reads=("xn1", "c_ident"), writes=("p6",))
                p.op("ve", lambda e, dt=dt, itl=itl: e.tensor_scalar(
                    out=B_(o_h2T + dt * 1024 + itl * 128, 128),
                    in0=pb[6][0:128, 0:128],
                    scalar1=afa[:, o_l2g + dt:o_l2g + dt + 1],
                    scalar2=afa[:, o_l2b + dt:o_l2b + dt + 1],
                    op0=ALU.mult, op1=ALU.add),
                    reads=("p6", "c_l2g", "c_l2b", "PH3"),
                    writes=(f"h2T{dt}_{itl}",))

            def mmr(e, itl=itl):
                e.matmul(out=pb[6][0:128, 0:8],
                         lhsT=B_(o_ones128, 128, 0, 1),
                         rhs=B_(o_brrow, 8, 0, 1), start=True, stop=False)
                for dt in range(8):
                    inst = e.matmul(
                        out=pb[6][0:128, 0:8],
                        lhsT=B_(o_h2T + dt * 1024 + itl * 128, 128),
                        rhs=B_(o_Wr + dt * 8, 8),
                        start=False, stop=(dt == 7))
                return inst
            p.op("te", mmr, reads=("c_ones128", "c_br", "c_Wr") +
                 tuple(f"h2T{d}_{itl}" for d in range(8)), writes=("p6",))
            r = o_red
            p.op("ve", lambda e: e.reduce_max(
                out=FA(r, 1), in_=pb[6][0:128, 0:8], axis=AX.X, negate=True),
                reads=("p6",), writes=("red0",))
            p.op("ve", lambda e: e.tensor_scalar(
                out=FA(o_sq, 8), in0=pb[6][0:128, 0:8], scalar1=FA(r, 1),
                scalar2=0.0, op0=ALU.add, op1=ALU.is_ge),
                reads=("p6", "red0"), writes=("sq",))
            p.op("ve", lambda e: e.scalar_tensor_tensor(
                out=FA(o_sq + 16, 8), in0=FA(o_sq, 8), scalar=-1e30,
                in1=pb[6][0:128, 0:8], op0=ALU.mult, op1=ALU.add),
                reads=("sq", "p6"), writes=("lm",))
            p.op("ve", lambda e: e.reduce_max(
                out=FA(r + 1, 1), in_=FA(o_sq + 16, 8), axis=AX.X,
                negate=True),
                reads=("lm",), writes=("red1",))
            p.op("se", lambda e: e.activation(
                out=FA(o_sq + 32, 8), in_=pb[6][0:128, 0:8], func=AF.Exp,
                bias=FA(r, 1)),
                reads=("p6", "red0"), writes=("rexp",))
            p.op("ve", lambda e: e.tensor_scalar(
                out=FA(o_sq + 48, 8), in0=pb[6][0:128, 0:8],
                scalar1=FA(r + 1, 1), scalar2=0.0, op0=ALU.add,
                op1=ALU.is_ge),
                reads=("p6", "red1"), writes=("km",))
            p.op("ve", lambda e: e.tensor_tensor(
                out=FA(o_sq + 32, 8), in0=FA(o_sq + 32, 8),
                in1=FA(o_sq + 48, 8), op=ALU.mult),
                reads=("rexp", "km"), writes=("rexp",))
            p.op("ve", lambda e: e.reduce_sum(
                out=FA(r + 2, 1), in_=FA(o_sq + 32, 8), axis=AX.X),
                reads=("rexp",), writes=("red2",))
            p.op("ve", lambda e: e.reciprocal(out=FA(r + 3, 1),
                                              in_=FA(r + 2, 1)),
                 reads=("red2",), writes=("red3",))
            p.op("ve", lambda e, itl=itl: e.tensor_scalar_mul(
                out=afa[:, o_wrt + itl * 8:o_wrt + itl * 8 + 8],
                in0=FA(o_sq + 32, 8), scalar1=FA(r + 3, 1)),
                reads=("rexp", "red3"), writes=(f"wrt{itl}",))
            p.op("te", lambda e, itl=itl: e.transpose(
                out=pb[6][0:8, 0:128],
                in_=afa[:, o_wrt + itl * 8:o_wrt + itl * 8 + 8],
                identity=FA(o_ident, 128)),
                reads=(f"wrt{itl}", "c_ident"), writes=("p6",))
            p.op("se", lambda e, itl=itl: e.copy(
                out=B_(o_wT + itl * 128, 128, 0, 8), in_=pb[6][0:8, 0:128]),
                reads=("p6", "PH3"), writes=(f"wTt{itl}",))
        for itl in range(8):
            for dch in range(2):
                p.op("te", lambda e, itl=itl, dch=dch: e.matmul(
                    out=pb[6][0:128, 0:512],
                    lhsT=B_(o_wT + itl * 128, 128, 0, 8),
                    rhs=B_(o_eb2 + dch * 512, 512, 0, 8),
                    start=True, stop=True),
                    reads=(f"wTt{itl}", "c_eb2"), writes=("p6",))
                p.op("ve", lambda e, itl=itl, dch=dch: e.tensor_copy(
                    out=FB(itl * 1024 + dch * 512, 512),
                    in_=pb[6][0:128, 0:512]),
                    reads=("p6", "PH3"), writes=(f"acc{itl}_{dch}",))
        SW1 = [o_xb[0], o_xb[1], o_sq, o_xc]
        SW1K = ["xb0", "xb1", "sq", "xc"]
        SW2 = [o_xn1, o_xn1 + 512, o_h1, o_h1 + 512]
        SW2K = ["w2s0", "w2s1", "w2s2", "w2s3"]
        for ee in range(E):
            for ft in range(32):
                wb = ft % 2
                sb1 = ft % 4
                p.dma("vq", lambda e, ee=ee, ft=ft, sb1=sb1: e.dma_start(
                    out=FA(SW1[sb1], 1024).rearrange(
                        "q (dt f) -> q dt f", f=128),
                    in_=ew1[ee, :, ft * 128:(ft + 1) * 128].rearrange(
                        "(dt q) f -> q dt f", q=128)),
                    reads=(), writes=(SW1K[sb1],))
                p.op("ve", lambda e, wb=wb, sb1=sb1: e.tensor_copy(
                    out=B_(o_w1b[wb], 1024), in_=FA(SW1[sb1], 1024)),
                    reads=(SW1K[sb1],), writes=(f"w1b{wb}",))
                for ch in range(2):
                    bank, bk = pb[ch], PK[ch]

                    def mm1(e, ft=ft, ch=ch, wb=wb, bank=bank):
                        for dt in range(8):
                            inst = e.matmul(
                                out=bank[0:128, 0:512],
                                lhsT=B_(o_w1b[wb] + dt * 128, 128),
                                rhs=B_(o_h2T + dt * 1024 + ch * 512, 512),
                                start=(dt == 0), stop=(dt == 7))
                        return inst
                    p.op("te", mm1, reads=(f"w1b{wb}",) + H2K, writes=(bk,))
                    p.op("se", lambda e, ft=ft, ch=ch, ee=ee, bank=bank:
                         e.activation(
                             out=B_(o_mid + ft * 1024 + ch * 512, 512),
                             in_=bank[0:128, 0:512], func=AF.Gelu,
                             bias=afa[:, o_b1c + ee * 32 + ft:
                                      o_b1c + ee * 32 + ft + 1]),
                         reads=(bk, "c_b1c", "PH3"),
                         writes=(f"mid{ft}_{ch}",))
                p.dma("vq", lambda e, ee=ee, ft=ft: e.dma_start(
                    out=FA(SW2[ft % 4], 512),
                    in_=ew2[ee, ft * 128:(ft + 1) * 128, 0:512]),
                    reads=(), writes=(SW2K[ft % 4],) +
                    (("xn1", "h1") if ft == 0 else ()))
                p.op("ve", lambda e, ft=ft: e.tensor_copy(
                    out=B_(o_w2b + ft * 512, 512),
                    in_=FA(SW2[ft % 4], 512)),
                    reads=(SW2K[ft % 4], "PH3"), writes=(f"w2b{ft}",))
            for dch in range(2):
                if dch == 1:
                    for ft in range(32):
                        p.dma("vq", lambda e, ee=ee, ft=ft: e.dma_start(
                            out=FA(SW2[ft % 4], 512),
                            in_=ew2[ee, ft * 128:(ft + 1) * 128, 512:1024]),
                            reads=(), writes=(SW2K[ft % 4],))
                        p.op("ve", lambda e, ft=ft: e.tensor_copy(
                            out=B_(o_w2b + ft * 512, 512),
                            in_=FA(SW2[ft % 4], 512)),
                            reads=(SW2K[ft % 4],), writes=(f"w2b{ft}",))
                for itl in range(8):
                    bank, bk = pb[2 + itl % 2], PK[2 + itl % 2]

                    def mm2(e, itl=itl, bank=bank):
                        for ft in range(32):
                            inst = e.matmul(
                                out=bank[0:128, 0:512],
                                lhsT=B_(o_mid + ft * 1024 + itl * 128, 128),
                                rhs=B_(o_w2b + ft * 512, 512),
                                start=(ft == 0), stop=(ft == 31))
                        return inst
                    p.op("te", mm2, reads=MIDK + W2K, writes=(bk,))
                    p.op("ve", lambda e, itl=itl, dch=dch, ee=ee, bank=bank:
                         e.scalar_tensor_tensor(
                             out=FB(itl * 1024 + dch * 512, 512),
                             in0=bank[0:128, 0:512],
                             scalar=afa[:, o_wrt + itl * 8 + ee:
                                        o_wrt + itl * 8 + ee + 1],
                             in1=FB(itl * 1024 + dch * 512, 512),
                             op0=ALU.mult, op1=ALU.add),
                         reads=(bk, f"wrt{itl}", f"acc{itl}_{dch}"),
                         writes=(f"acc{itl}_{dch}",))
        for itl in range(8):
            it = half * 8 + itl
            r0 = 256 * it
            p.dma("sy", lambda e, it=it: e.dma_start(
                out=FA(o_xb[0], 1024), in_=s_a[it, :, :]),
                reads=(f"sa{it}",), writes=("xb0",))
            p.dma("sy", lambda e, r0=r0: e.dma_start(
                out=FA(o_xb[1], 1024), in_=x[r0:r0 + 128, :]),
                reads=(), writes=("xb1",))
            p.op("ve", lambda e, itl=itl: e.tensor_tensor(
                out=FA(o_xc, 1024), in0=FB(itl * 1024, 1024),
                in1=FA(o_xb[0], 1024), op=ALU.add),
                reads=(f"acc{itl}_0", f"acc{itl}_1", "xb0"), writes=("xc",))
            p.op("ve", lambda e: e.tensor_tensor(
                out=FA(o_xc, 1024), in0=FA(o_xc, 1024),
                in1=FA(o_xb[1], 1024), op=ALU.add),
                reads=("xc", "xb1"), writes=("xc",))
            p.dma("sy", lambda e, it=it: e.dma_start(
                out=s_a[it, :, :], in_=FA(o_xc, 1024)),
                reads=("xc",), writes=(f"sa{it}",))
            if DBG:
                p.dma("sy", lambda e, it=it: e.dma_start(
                    out=dbg_outs["d_part"][it, :, :], in_=FA(o_xc, 1024)),
                    reads=("xc",), writes=(f"dp{it}",))
        if DBG and half == 1:
            p.dma("sy", lambda e: e.dma_start(
                out=dbg_outs["d_w"][:, :], in_=FA(o_wrt, 64)),
                reads=tuple(f"wrt{i}" for i in range(8)), writes=())

    p.op("ve", lambda e: e.memset(FA(o_red + 7, 1), 0.0),
         reads=(), writes=MIDK + H2K + W2K +
         tuple(f"w1b{i}" for i in range(2)) +
         tuple(f"wTt{i}" for i in range(8)) + ("PH4",))

    load_w_pass(Wproj, 0, rdeps=("PH4",))
    for h in range(H):
        hb = h % 2
        p.dma("sy", lambda e, h=h, hb=hb: e.dma_start(
            out=abf[0:66, o_kh_b[hb]:o_kh_b[hb] + 2048], in_=s_kh[h, :, :]),
            reads=("PH4",), writes=(f"khb{hb}",))
        p.dma("sy", lambda e, h=h, hb=hb: e.dma_start(
            out=abf[0:66, o_qh_b[hb]:o_qh_b[hb] + 4096], in_=s_qr[h, :, :]),
            reads=("PH4",), writes=(f"qhb{hb}",))
        p.dma("sy", lambda e, h=h, hb=hb: e.dma_start(
            out=abf[:, o_vrb[hb]:o_vrb[hb] + 2048].rearrange(
                "q (j n) -> q j n", n=64),
            in_=s_vr[h // 8][:, :, (h % 8) * 64:(h % 8) * 64 + 64].rearrange(
                "j q n -> q j n")),
            reads=("PH4",), writes=(f"vrb{hb}",))
        for oc in range(4):
            pvbank, pvk = pb[4 + oc % 2], PK[4 + oc % 2]
            for Jp in range(16):
                J0 = 2 * Jp

                def mmD(e, J0=J0, hb=hb, oc=oc):
                    for dj in range(2):
                        inst = e.matmul(
                            out=pb23[0:128, dj * 512:dj * 512 + 512],
                            lhsT=abf[0:66, o_qh_b[hb] + (J0 + dj) * 128:
                                     o_qh_b[hb] + (J0 + dj) * 128 + 128],
                            rhs=abf[0:66, o_kh_b[hb] + oc * 512:
                                    o_kh_b[hb] + oc * 512 + 512],
                            start=True, stop=True)
                    return inst
                p.op("te", mmD, reads=(f"qhb{hb}", f"khb{hb}"),
                     writes=("p2", "p3"))
                p.op("se", lambda e: e.activation(
                    out=B_(o_eb[0], 1024), in_=pb23[0:128, 0:1024],
                    func=AF.Exp),
                    reads=("p2", "p3", "PH4"), writes=("ebf",))

                def mmPVr(e, J0=J0, hb=hb, pvbank=pvbank):
                    for dj in range(2):
                        J = J0 + dj
                        inst = e.matmul(
                            out=pvbank[0:64, 0:512],
                            lhsT=abf[:, o_vrb[hb] + J * 64:
                                     o_vrb[hb] + J * 64 + 64],
                            rhs=B_(o_eb[0] + dj * 512, 512),
                            start=(J == 0), stop=(J == 31))
                    return inst
                p.op("te", mmPVr, reads=(f"vrb{hb}", "ebf"), writes=(pvk,))
            p.op("se", lambda e, pvbank=pvbank: e.copy(
                out=B_(o_bfst[0], 512, 0, 64), in_=pvbank[0:64, 0:512]),
                reads=(pvk, "PH4"), writes=("bfst0",))
            p.dma("sy", lambda e, h=h, oc=oc: e.dma_start(
                out=abf[64 * (h % 2):64 * (h % 2) + 64,
                        o_orT + (h // 2) * 2048 + oc * 512:
                        o_orT + (h // 2) * 2048 + oc * 512 + 512],
                in_=B_(o_bfst[0], 512, 0, 64)),
                reads=("bfst0", "PH4"), writes=(f"orT{h}_{oc}",))
    ORT = tuple(f"orT{h}_{oc}" for h in range(H) for oc in range(4))

    if DBG:
        p.dma("sy", lambda e: e.dma_start(
            out=dbg_outs["d_orT"][:, :], in_=abf[:, o_orT:o_orT + 16384]),
            reads=ORT, writes=())

    for it in range(NB):
        xk = f"xb{it % 2}"
        p.dma("sy", lambda e, it=it: e.dma_start(
            out=FA(o_xb[it % 2], 1024), in_=s_a[it, :, :]),
            reads=(f"sa{it}",), writes=(xk,))
        for nc2 in range(2):
            bank, bk = pb[nc2], PK[nc2]

            def mmf(e, it=it, nc2=nc2, bank=bank):
                e.matmul(out=bank[0:128, 0:512],
                         lhsT=B_(o_ones128, 128, 0, 1),
                         rhs=B_(o_rows2 + nc2 * 512, 512, 0, 1),
                         start=True, stop=False)
                for dt in range(8):
                    inst = e.matmul(
                        out=bank[0:128, 0:512],
                        lhsT=B_(o_orT + dt * 2048 + it * 128, 128),
                        rhs=B_(o_wsb + dt * 1024 + nc2 * 512, 512),
                        start=False, stop=(dt == 7))
                return inst
            p.op("te", mmf, reads=ORT + WK + ("c_ones128", "c_bp"),
                 writes=(bk,))
            p.op("ve", lambda e, it=it, nc2=nc2, bank=bank: e.tensor_tensor(
                out=afa[:, o_xc + nc2 * 512:o_xc + nc2 * 512 + 512],
                in0=bank[0:128, 0:512],
                in1=afa[:, o_xb[it % 2] + nc2 * 512:
                        o_xb[it % 2] + nc2 * 512 + 512],
                op=ALU.add),
                reads=(bk, xk), writes=("xc",))
        p.dma("sy", lambda e, it=it: e.dma_start(
            out=out[it * 128:(it + 1) * 128, :], in_=FA(o_xc, 1024)),
            reads=("xc",), writes=())

    p.finish()
    p.emit(nc, sems)
    return st


def kernel(**inputs):
    gi = {k: np.asarray(v) for k, v in inputs.items()}
    x_full = np.ascontiguousarray(gi["L0_attn_x"], dtype=np.float32)

    def cols(v, k):
        return np.ascontiguousarray(
            np.asarray(v, np.float32).reshape(k, 128).T)

    rsig = np.asarray(gi["L0_attn_r_sigma"], np.float32).reshape(1, 1)
    common = {
        "Wqkv": np.ascontiguousarray(gi["L0_attn_Wqkv"], np.float32),
        "Wq": np.ascontiguousarray(gi["L0_ffn_Wq"], np.float32),
        "Wk": np.ascontiguousarray(gi["L0_ffn_Wk"], np.float32),
        "Wv": np.ascontiguousarray(gi["L0_ffn_Wv"], np.float32),
        "Wo": np.ascontiguousarray(gi["L0_ffn_Wo"], np.float32),
        "Wproj": np.ascontiguousarray(gi["L0_attn_Wproj"], np.float32),
        "Wr": np.ascontiguousarray(gi["L0_ffn_Wr"], np.float32),
        "ew1": np.ascontiguousarray(gi["L0_ffn_e_w1"], np.float32),
        "ew2": np.ascontiguousarray(gi["L0_ffn_e_w2"], np.float32),
        "eb2": np.ascontiguousarray(gi["L0_ffn_e_b2"], np.float32),
        "bqkv_c": cols(gi["L0_attn_bqkv"], 24),
        "bq_c": cols(gi["L0_ffn_bq"], 8),
        "bk_c": cols(gi["L0_ffn_bk"], 8),
        "l1g_c": cols(gi["L0_ffn_ln1_g"], 8),
        "l1b_c": cols(gi["L0_ffn_ln1_b"], 8),
        "l2g_c": cols(gi["L0_ffn_ln2_g"], 8),
        "l2b_c": cols(gi["L0_ffn_ln2_b"], 8),
        "b1cols": np.ascontiguousarray(
            np.asarray(gi["L0_ffn_e_b1"], np.float32).reshape(8, 32, 128)
            .transpose(2, 0, 1).reshape(128, 256)),
        "bqkv_vrow": np.ascontiguousarray(
            np.asarray(gi["L0_attn_bqkv"], np.float32)[2048:3072]
            .reshape(1, 1024)),
        "bv_row": np.ascontiguousarray(
            np.asarray(gi["L0_ffn_bv"], np.float32).reshape(1, 1024)),
        "bo_row": np.ascontiguousarray(
            np.asarray(gi["L0_ffn_bo"], np.float32).reshape(1, 1024)),
        "bproj_row": np.ascontiguousarray(
            np.asarray(gi["L0_attn_bproj"], np.float32).reshape(1, 1024)),
        "br_row": np.ascontiguousarray(
            np.asarray(gi["L0_ffn_br"], np.float32).reshape(1, 8)),
        "ident": np.eye(128, dtype=np.float32),
        "ones_colf": np.ones((128, 1), np.float32),
        "ones_rowf": np.ones((1, 128), np.float32),
        "ones_bf": np.ones((1, 2048), ml_dtypes.bfloat16),
        "ones128_bf": np.ones((1, 128), ml_dtypes.bfloat16),
        "rsig_bc": np.ascontiguousarray(np.broadcast_to(rsig, (128, 1))),
    }

    def mk_masks(parity):
        m = np.zeros((8, 128, 512), np.float32)
        tri = np.triu(np.ones((128, 128), np.float32))
        for r in range(8):
            for c in range(4):
                t = 2 * c
                if r < t or (parity == 1 and r == t + 1):
                    m[r, :, c * 128:(c + 1) * 128] = 1.0
                elif r == t:
                    m[r, :, c * 128:(c + 1) * 128] = tri
        return np.ascontiguousarray(m.astype(ml_dtypes.bfloat16))

    masks_by_par = [mk_masks(0), mk_masks(1)]

    in_maps = []
    for c in range(8):
        b, par = c // 2, c % 2
        d = dict(common)
        xb = x_full[b]
        if par == 1:
            xb = xb.reshape(16, 2, 128, 1024)[:, ::-1].reshape(4096, 1024)
        d["x"] = np.ascontiguousarray(xb)
        d["masks"] = masks_by_par[par]
        in_maps.append(d)

    nc = bass.Bass("TRN2", target_bir_lowering=False)
    stack = build_program(nc)
    res = run_bass_kernel_spmd(nc, in_maps, core_ids=list(range(8)))
    stack.close()

    outp = np.empty((4, 4096, 1024), np.float32)
    for c in range(8):
        b, par = c // 2, c % 2
        o = np.asarray(res.results[c]["out"]).reshape(16, 128, 1024)
        outp[b].reshape(16, 2, 128, 1024)[:, par] = o
    return outp
